# version 08
# baseline (speedup 1.0000x reference)
import math
import numpy as np

import concourse.bass as bass
import concourse.mybir as mybir

F32 = mybir.dt.float32
BF16 = mybir.dt.bfloat16
I32 = mybir.dt.int32
AF = mybir.ActivationFunctionType
ALU = mybir.AluOpType
AX = mybir.AxisListType

T = 4096
D = 1024
NB = T // 128
H = 16
HD = 64
HIDDEN = 3072
LAT = 256
E = 8
TOWN = T // 2
NBO = TOWN // 128
QSCALE = 0.125
DEQ = 1.0 / 127.0
M_L1 = 2.0
NEG_BIG = -1.0e6
EPS_LN = 1e-5
EPS_TIE = 1e-6


class Ch:
    def __init__(self, nc, name):
        self._cm = nc.semaphore(name)
        self.sem = self._cm.__enter__()
        self.n = 0


class P:
    def __init__(self, nc):
        self.nc = nc
        self.ops = {"tensor": [], "vector": [], "scalar": [], "sync": [], "gpsimd": []}
        self.eng_ch = {}
        self.dma_chs = set()

    def emit(self, eng, fn):
        self.ops[eng].append(fn)

    def op(self, eng, body, waits=(), inc=None, inc_k=1, self_wait=True):
        waits = [(ch, ch.n if ch in self.dma_chs else v) for (ch, v) in waits]
        ch0 = self.eng_ch.get(eng)
        if ch0 is not None:
            if self_wait and ch0.n > 0:
                waits.append((ch0, ch0.n))
            if inc is None:
                inc = ch0
            assert inc is ch0, f"{eng} ops must inc their engine channel"
        def fn(e, waits=waits, body=body, inc=inc, inc_k=inc_k):
            for ch, v in waits:
                e.wait_ge(ch.sem, v)
            inst = body(e)
            if inc is not None:
                assert inst is not None, "op body must return its instruction"
                inst.then_inc(inc.sem, inc_k)
        if inc is not None:
            inc.n += inc_k
        self.emit(eng, fn)
        return inc.n if inc is not None else None

    def wait(self, eng, ch, v=None):
        vv = ch.n if v is None else v
        self.emit(eng, lambda e: e.wait_ge(ch.sem, vv))

    def mark(self, label):
        if not hasattr(self, "marks"):
            self.marks = []
        self.marks.append((label, {k: len(v) for k, v in self.ops.items()}))


def build(debug=0):
    nc = bass.Bass("TRN2", target_bir_lowering=False)
    TG = 512
    NTG = T // TG

    xT = nc.dram_tensor("xT", [D, T], F32, kind="ExternalInput").ap()
    w_qT = nc.dram_tensor("w_qT", [D, D], F32, kind="ExternalInput").ap()
    w_kT = nc.dram_tensor("w_kT", [D, D], F32, kind="ExternalInput").ap()
    w_vT = nc.dram_tensor("w_vT", [D, D], F32, kind="ExternalInput").ap()
    w_outT = nc.dram_tensor("w_outT", [D, D], F32, kind="ExternalInput").ap()
    wgT = nc.dram_tensor("wgT", [D, HIDDEN], I32, kind="ExternalInput").ap()
    wuT = nc.dram_tensor("wuT", [D, HIDDEN], I32, kind="ExternalInput").ap()
    wdT = nc.dram_tensor("wdT", [HIDDEN, D], I32, kind="ExternalInput").ap()
    tri = nc.dram_tensor("tri", [128, 128], F32, kind="ExternalInput").ap()
    ident_in = nc.dram_tensor("ident", [128, 128], F32, kind="ExternalInput").ap()
    Wdq = nc.dram_tensor("Wdq", [256, D], F32, kind="ExternalInput").ap()
    WdqT = nc.dram_tensor("WdqT", [D, 256], F32, kind="ExternalInput").ap()
    WuqT = nc.dram_tensor("WuqT", [256, D], F32, kind="ExternalInput").ap()
    Wuq = nc.dram_tensor("Wuq", [D, 256], F32, kind="ExternalInput").ap()
    Wuk = nc.dram_tensor("Wuk", [D, 256], F32, kind="ExternalInput").ap()
    Wuv = nc.dram_tensor("Wuv", [D, 256], F32, kind="ExternalInput").ap()
    WoT = nc.dram_tensor("WoT", [D, D], F32, kind="ExternalInput").ap()
    WdkvT = nc.dram_tensor("WdkvT", [D, 256], F32, kind="ExternalInput").ap()
    WgT_moe = nc.dram_tensor("WgT_moe", [D, E], F32, kind="ExternalInput").ap()
    WeT = nc.dram_tensor("WeT", [E, D, D], F32, kind="ExternalInput").ap()
    be_in = nc.dram_tensor("be", [E, D], F32, kind="ExternalInput").ap()
    mkA = nc.dram_tensor("mkA", [128, 128], F32, kind="ExternalInput").ap()
    mkB = nc.dram_tensor("mkB", [128, 128], F32, kind="ExternalInput").ap()
    u_in = nc.dram_tensor("u_in", [128, 1], F32, kind="ExternalInput").ap()
    sel2_in = nc.dram_tensor("sel2_in", [2, 128], F32, kind="ExternalInput").ap()
    eps_row_in = nc.dram_tensor("eps_row_in", [1, E], F32, kind="ExternalInput").ap()

    out = nc.dram_tensor("out", [D, TOWN], F32, kind="ExternalOutput").ap()
    dbg = {}
    if debug >= 1:
        dbg["xn"] = nc.dram_tensor("dbg_xn", [8, 128, T], F32, kind="ExternalOutput").ap()
    if debug >= 2:
        dbg["ctx"] = nc.dram_tensor("dbg_ctx", [8, 128, T], F32, kind="ExternalOutput").ap()
    if debug >= 3:
        dbg["x1o"] = nc.dram_tensor("dbg_x1o", [8, 128, TOWN], F32, kind="ExternalOutput").ap()
        dbg["ckv"] = nc.dram_tensor("dbg_ckv", [32, 128, LAT], F32, kind="ExternalOutput").ap()
        dbg["q"] = nc.dram_tensor("dbg_q", [8, 128, TOWN], F32, kind="ExternalOutput").ap()
        dbg["y"] = nc.dram_tensor("dbg_y", [8, 128, TOWN], F32, kind="ExternalOutput").ap()
    if debug >= 4:
        dbg["g1"] = nc.dram_tensor("dbg_g1", [8, 128, TOWN], F32, kind="ExternalOutput").ap()

    prog_tensors = []

    def sb(name, shape, dt):
        cm = nc.sbuf_tensor(name, shape, dt)
        t = cm.__enter__()
        prog_tensors.append(cm)
        return t

    def ps(name, shape, dt=F32):
        cm = nc.psum_tensor(name, shape, dt)
        t = cm.__enter__()
        prog_tensors.append(cm)
        return t

    A = sb("A", [128, 8 * T], BF16)
    B = sb("B", [128, 8 * T], BF16)
    KQ = sb("KQ", [128, 2 * T], BF16)
    VT = sb("VT", [128, T], BF16)
    PT = sb("PT", [128, 2 * 33 * 128], BF16)
    WS = sb("WS", [128, 2 * 1024], BF16)
    XS = sb("XS", [128, 8 * TG], F32)
    SM = sb("SM", [128, 3072], F32)
    CST = sb("CST", [128, 800], F32)
    CSTB = sb("CSTB", [128, 256], BF16)
    PB = XS.bitcast(BF16)[:, 0:T]

    pbank = [ps(f"pb{i}", [128, 512]) for i in range(8)]

    ident_f = CST[:, 0:128]
    tri_f = CST[:, 128:256]
    mkA_f = CST[:, 256:384]
    mkB_f = CST[:, 384:512]
    ones_col = CST[:, 512:513]
    u_col = CST[:, 513:514]
    zero_col = CST[:, 514:515]
    negml1_col = CST[:, 515:516]
    ones_row = CST[0:1, 520:648]
    sel2 = CST[0:2, 648:776]
    ident_b = CSTB[:, 0:128]
    ones_col_b = CSTB[:, 128:129]

    p = P(nc)
    ld = Ch(nc, "ld")
    tsp = Ch(nc, "tsp")
    tspA = Ch(nc, "tspA")
    tspB = Ch(nc, "tspB")
    tn = Ch(nc, "tn")
    vp = Ch(nc, "vp")
    sp = Ch(nc, "sp")
    gd = Ch(nc, "gd")
    gdA = Ch(nc, "gdA")
    gdB = Ch(nc, "gdB")
    so = Ch(nc, "so")
    sod = Ch(nc, "sod")
    p.eng_ch = {"vector": vp, "scalar": sp}
    p.dma_chs = {ld, tsp, tspA, tspB, gd, gdA, gdB, so, sod}

    p.op("gpsimd", lambda e: e.dma_start(out=ident_f, in_=ident_in), inc=gd, inc_k=16)
    p.op("gpsimd", lambda e: e.dma_start(out=ident_b, in_=ident_in), inc=gd, inc_k=16)
    p.op("gpsimd", lambda e: e.dma_start(out=tri_f, in_=tri), inc=gd, inc_k=16)
    p.op("gpsimd", lambda e: e.dma_start(out=mkA_f, in_=mkA), inc=gd, inc_k=16)
    p.op("gpsimd", lambda e: e.dma_start(out=mkB_f, in_=mkB), inc=gd, inc_k=16)
    p.op("gpsimd", lambda e: e.dma_start(out=u_col, in_=u_in), inc=gd, inc_k=16)
    p.op("gpsimd", lambda e: e.dma_start(out=sel2, in_=sel2_in), inc=gd, inc_k=16)
    consts_gd = gd.n
    p.op("vector", lambda e: e.memset(ones_col, 1.0), inc=vp)
    p.op("vector", lambda e: e.memset(ones_col_b, 1.0), inc=vp)
    p.op("vector", lambda e: e.memset(zero_col, 0.0), inc=vp)
    p.op("vector", lambda e: e.memset(negml1_col, -M_L1), inc=vp)
    p.op("vector", lambda e: e.memset(ones_row, 1.0), inc=vp)
    consts_v = vp.n
    p.wait("tensor", gd, consts_gd)
    p.wait("scalar", gd, consts_gd)
    p.wait("vector", gd, consts_gd)

    def xn_c(c):
        return A[:, c * T:(c + 1) * T]

    sq_buf = [SM[:, 0:512], SM[:, 512:1024]]
    tmpA = SM[:, 1024:1536]
    r1 = SM[0:1, 1536:2048]
    r2 = SM[0:1, 2048:2560]
    r3 = SM[0:1, 2560:3072]

    prev_consumed = 0
    for g in range(NTG):
        xsv = XS[:].rearrange("p (c t) -> p c t", c=8)
        src = xT.rearrange("(c p) t -> p c t", p=128)[:, :, g * TG:(g + 1) * TG]
        ldv = p.op("sync", lambda e, s=src, d=xsv: e.dma_start(out=d, in_=s),
                   waits=[(vp, prev_consumed)] if g > 0 else (), inc=ld, inc_k=16)
        for c in range(8):
            p.op("tensor", lambda e, c=c, xsv=xsv: e.matmul(
                pbank[6][0:1, 0:TG], ones_col, xsv[:, c, :], start=(c == 0), stop=(c == 7)),
                waits=[(ld, ldv), (vp, consts_v)] if c == 0 else (), inc=tn)
        t_sum = tn.n
        sq_mm = {}
        for c in range(8):
            sq = sq_buf[c % 2]
            w = [(ld, ldv)] if c == 0 else []
            if c >= 2:
                w.append((tn, sq_mm[c - 2]))
            sv = p.op("scalar", lambda e, c=c, xsv=xsv, sq=sq: e.square(sq, xsv[:, c, :]),
                      waits=w, inc=sp)
            sq_mm[c] = p.op("tensor", lambda e, c=c, sq=sq: e.matmul(
                pbank[7][0:1, 0:TG], ones_col, sq, start=(c == 0), stop=(c == 7)),
                waits=[(sp, sv), (tn, t_sum)] if c == 0 else [(sp, sv)], inc=tn)
        t_sq = tn.n

        p.op("vector", lambda e: e.tensor_scalar(r1, pbank[6][0:1, 0:TG], 1.0 / D, None, ALU.mult),
             waits=[(tn, t_sq)])
        p.op("vector", lambda e: e.tensor_scalar(r2, pbank[7][0:1, 0:TG], 1.0 / D, None, ALU.mult))
        p.op("vector", lambda e: e.tensor_tensor(r3, r1, r1, ALU.mult))
        p.op("vector", lambda e: e.tensor_tensor(r2, r2, r3, ALU.subtract))
        p.op("vector", lambda e: e.tensor_scalar(r2, r2, EPS_LN, None, ALU.add))
        v_stats = vp.n
        p.op("scalar", lambda e: e.sqrt(r2, r2), waits=[(vp, v_stats)], inc=sp)
        s_sqrt = sp.n

        p.op("vector", lambda e: e.reciprocal(r2, r2), waits=[(sp, s_sqrt)])
        p.op("vector", lambda e: e.tensor_tensor(r1, r1, r2, ALU.mult))
        p.op("vector", lambda e: e.tensor_scalar(r1, r1, -1.0, None, ALU.mult))
        v_ab = vp.n
        p.op("tensor", lambda e: e.matmul(pbank[4][:, 0:TG], ones_row, r2, start=True, stop=True),
             waits=[(vp, v_ab)], inc=tn)
        p.op("tensor", lambda e: e.matmul(pbank[5][:, 0:TG], ones_row, r1, start=True, stop=True),
             inc=tn)
        t_bc = tn.n
        for c in range(8):
            p.op("vector", lambda e, c=c, xsv=xsv: e.tensor_tensor(
                tmpA, xsv[:, c, :], pbank[4][:, 0:TG], ALU.mult),
                waits=[(tn, t_bc)] if c == 0 else ())
            p.op("vector", lambda e, c=c, g=g: e.tensor_tensor(
                xn_c(c)[:, g * TG:(g + 1) * TG], tmpA, pbank[5][:, 0:TG], ALU.add))
        prev_consumed = vp.n
        if g < NTG - 1:
            p.wait("tensor", vp, prev_consumed)
    v_xn_done = vp.n

    if debug >= 1:
        for c in range(8):
            p.op("gpsimd", lambda e, c=c: e.dma_start(out=dbg["xn"][c], in_=xn_c(c)),
                 waits=[(vp, v_xn_done)], inc=sod, inc_k=16)

    p.mark("A_ln")
    XSB = XS.bitcast(BF16)
    SMB = SM.bitcast(BF16)
    KA = [KQ[:, 0:T], KQ[:, T:2 * T]]
    QA = [VT[:, 0:T], XSB[:, T:2 * T]]
    PB2 = [XSB[:, 0:512], XSB[:, 512:1024]]
    VTA = PT[:, 0:T]
    PBK = [[PT[:, T:T + 512], PT[:, T + 512:T + 1024]],
           [PT[:, T + 1024:T + 1536], PT[:, T + 1536:T + 2048]]]
    VSufR = [PT[0:1, 6144:8192], PT[64:65, 6144:8192]]
    EMR = [SMB[0:1, 2048:6144], SMB[64:65, 2048:6144]]
    Mt = SM[:, 0:64]
    mtmp = SM[:, 64:65]
    mtcols = SM[:, 1004:1012]
    negMtb = SMB[:, 132:196]
    negMtf = SM[:, 100:164]
    EMtf = SM[:, 164:228]
    stackf = SM[:, 100:228]
    SMt = SM[:, 100:228]
    triT_f = SM[:, 228:356]
    VSb2 = [SM[0:64, 356:389], SM[0:64, 389:422]]
    VSuf2 = [SM[0:64, 422:455], SM[0:64, 455:488]]
    VSufT = [SM[0:32, 488:552], SM[0:32, 552:616]]
    zrow512b = SMB[0:1, 1232:1744]
    ones64f = SM[64:65, 872:1000]
    onesrb = SMB[0:1, 1744:1872]
    zbcs = XS[:, 512:1024]
    Zfin = [XS[0:1, 1024:1536], XS[64:65, 1024:1536]]
    zbc = SM[:, 640:768]
    zr2 = SM[0:2, 1792:1920]
    wbuf = [WS[:, 0:1024].rearrange("p (c e) -> p c e", c=8),
            WS[:, 1024:2048].rearrange("p (c e) -> p c e", c=8)]

    def ctx_c(hg):
        return B[:, hg * T:(hg + 1) * T]

    def load_whg(wsrc, hg, slot, waits=()):
        src = wsrc.rearrange("(c p) e -> p c e", p=128)[:, :, hg * 128:(hg + 1) * 128]
        chn = gdA if slot == 0 else gdB
        p.op("gpsimd", lambda e, s=src, d=wbuf[slot]: e.dma_start(out=d, in_=s),
             waits=waits, inc=chn, inc_k=16)
        return chn

    if debug >= 5:
        p.op("vector", lambda e: e.memset(KQ[:, 0:2 * T], 0.0),
             waits=[(vp, v_xn_done), (sp, sp.n)], inc=vp)
        p.op("vector", lambda e: e.memset(PT[:, 0:8448], 0.0), inc=vp)
        p.op("vector", lambda e: e.memset(SM[:, 0:3072], 0.0), inc=vp)
        p.op("vector", lambda e: e.memset(VT[:, 0:T], 0.0), inc=vp)
        p.op("vector", lambda e: e.memset(XS[:, 0:4096], 0.0), inc=vp)
    p.op("vector", lambda e: e.memset(KQ[64:65, 0:2 * T], 1.0),
         waits=[(vp, v_xn_done), (sp, sp.n)], inc=vp)
    p.op("vector", lambda e: e.memset(zrow512b, 0.0), inc=vp)
    p.op("vector", lambda e: e.memset(ones64f, 1.0), inc=vp)
    p.op("vector", lambda e: e.memset(onesrb, 1.0), inc=vp)
    v_binit0 = vp.n
    t_tri = p.op("tensor", lambda e: e.transpose(pbank[5][:, 0:128], tri_f, ident_f),
                 waits=[(vp, v_binit0)], inc=tn)
    p.op("vector", lambda e: e.tensor_copy(triT_f, pbank[5][:, 0:128]),
         waits=[(tn, t_tri)], inc=vp)
    v_binit = vp.n

    stgKQ = [SMB[64:128, 712:1224], SMB[64:128, 1232:1744]]

    def proj_split(wslot, wld, dst0, dst1, first_waits):
        ev_by_tg = []
        for tg in range(NTG):
            bank = pbank[4 + 2 * (tg % 2)]
            for c in range(8):
                wts = []
                if c == 0:
                    if tg == 0:
                        wts = [(wld, 0)] + list(first_waits)
                    elif tg >= 2:
                        wts = [(sp, ev_by_tg[tg - 2])]
                p.op("tensor", lambda e, c=c, tg=tg, bank=bank, wslot=wslot: e.matmul(
                    bank[:, 0:TG], wbuf[wslot][:, c, :],
                    xn_c(c)[:, tg * TG:(tg + 1) * TG], start=(c == 0), stop=(c == 7)),
                    waits=wts, inc=tn)
            t_mm = tn.n
            p.op("scalar", lambda e, tg=tg, bank=bank, dst0=dst0: e.copy(
                dst0(tg), bank[0:64, 0:TG]), waits=[(tn, t_mm)], inc=sp)
            ev1 = p.op("scalar", lambda e, tg=tg, bank=bank: e.copy(
                stgKQ[tg % 2], bank[64:128, 0:TG]), waits=[(gd, 0)], inc=sp)
            p.op("gpsimd", lambda e, tg=tg, dst1=dst1: e.dma_start(
                out=dst1(tg), in_=stgKQ[tg % 2]), waits=[(sp, ev1)], inc=gd, inc_k=16)
            ev_by_tg.append(sp.n)
        return sp.n

    prev_q_done = 0
    prev_v_done = 0
    prev_hg_exp = 0
    prev_hg_tn = 0
    b6_free = b7_free = 0
    av_free = {0: 0, 1: 0}
    last_exp = {}
    last_av = {}
    for hg in range(8):
        ldk = load_whg(w_kT, hg, 0, waits=[(sp, prev_q_done)])
        k_done = proj_split(0, ldk,
                            lambda tg: KA[0][0:64, tg * TG:(tg + 1) * TG],
                            lambda tg: KA[1][0:64, tg * TG:(tg + 1) * TG],
                            [(vp, v_xn_done), (vp, av_free[1]), (vp, v_binit)])
        ldv = load_whg(w_vT, hg, 1, waits=[(sp, prev_v_done)])
        ev_by_tg = []
        tr_by_tg = []
        for tg in range(NTG):
            bank = pbank[4 + (tg % 2)]
            for c in range(8):
                wts = []
                if c == 0:
                    if tg == 0:
                        wts = [(ldv, 0), (sp, k_done)]
                    elif tg >= 2:
                        wts = [(sp, ev_by_tg[tg - 2])]
                p.op("tensor", lambda e, c=c, tg=tg, bank=bank: e.matmul(
                    bank[:, 0:TG], wbuf[1][:, c, :],
                    xn_c(c)[:, tg * TG:(tg + 1) * TG], start=(c == 0), stop=(c == 7)),
                    waits=wts, inc=tn)
            t_mm = tn.n
            evw = [(tn, t_mm)]
            if tg >= 2:
                evw.append((tsp, tr_by_tg[tg - 2]))
            ev = p.op("scalar", lambda e, tg=tg, bank=bank: e.copy(
                PB2[tg % 2], bank[:, 0:TG]), waits=evw, inc=sp)
            ev_by_tg.append(ev)
            w_tr = [(sp, ev)]
            if tg == 0:
                w_tr.append((tn, prev_hg_tn))
            p.op("sync", lambda e, tg=tg, pb=PB2[tg % 2]: e.dma_start_transpose(
                VTA.rearrange("p (t x) -> p t x", x=128)[:, 4 * tg:4 * tg + 4, 0:64],
                pb[0:64, :]), waits=w_tr, inc=tsp, inc_k=16)
            p.op("sync", lambda e, tg=tg, pb=PB2[tg % 2]: e.dma_start_transpose(
                VTA.rearrange("p (t x) -> p t x", x=128)[:, 4 * tg:4 * tg + 4, 64:128],
                pb[64:128, :]), inc=tsp, inc_k=16)
            tr_by_tg.append(tsp.n)
        v_done = sp.n
        t_vtr = tsp.n
        ldq = load_whg(w_qT, hg, 0, waits=[(sp, k_done)])
        q_done = proj_split(0, ldq,
                            lambda tg: QA[0][0:64, tg * TG:(tg + 1) * TG],
                            lambda tg: QA[1][0:64, tg * TG:(tg + 1) * TG],
                            [(sp, v_done)])

        for kb in range(NB):
            for h in range(2):
                wts = []
                if kb == 0 and h == 0:
                    wts = [(tsp, t_vtr), (sp, prev_hg_exp)]
                col = kb if h == 0 else 40 + kb
                p.op("tensor", lambda e, kb=kb, h=h, col=col: e.matmul(
                    pbank[0][0:64, col:col + 1],
                    VTA[:, kb * 128 + 64 * h:kb * 128 + 64 * h + 64], ones_col_b,
                    start=True, stop=True), waits=wts, inc=tn)
        t_vs = tn.n
        p.op("vector", lambda e: e.tensor_copy(VSb2[0][:, 0:32], pbank[0][0:64, 0:32]),
             waits=[(tn, t_vs)], inc=vp)
        p.op("vector", lambda e: e.tensor_copy(VSb2[1][:, 0:32], pbank[0][0:64, 40:72]), inc=vp)
        for h in range(2):
            p.op("vector", lambda e, h=h: e.memset(VSuf2[h][:, 0:1], 0.0), inc=vp)
            p.op("vector", lambda e, h=h: e.tensor_copy(
                VSuf2[h][:, 31:32], VSb2[h][:, 31:32]), inc=vp)
            for i in range(30, 0, -1):
                p.op("vector", lambda e, h=h, i=i: e.tensor_add(
                    VSuf2[h][:, i:i + 1], VSb2[h][:, i:i + 1], VSuf2[h][:, i + 1:i + 2]),
                    inc=vp)
        v_vsuf = vp.n
        for h in range(2):
            tvs = p.op("tensor", lambda e, h=h: e.transpose(
                pbank[1][0:32, h * 64:h * 64 + 64], VSuf2[h][:, 0:32], ident_f[0:64, 0:64]),
                waits=[(vp, v_vsuf)] if h == 0 else (), inc=tn)
            p.op("vector", lambda e, h=h: e.tensor_copy(
                VSufT[h], pbank[1][0:32, h * 64:h * 64 + 64]), waits=[(tn, tvs)], inc=vp)
        v_vsr = vp.n
        for h in range(2):
            p.op("gpsimd", lambda e, h=h: e.dma_start(
                out=VSufR[h].rearrange("p (t c) -> p t c", t=32), in_=VSufT[h]),
                waits=[(vp, v_vsr), (tn, prev_hg_tn)], inc=gd, inc_k=16)
        g_vsr = gd.n

        last_p1 = {}
        for qb in range(NB):
            nkg = (qb + 1 + 3) // 4
            for h in range(2):
                qsl = QA[h][0:64, qb * 128:(qb + 1) * 128]
                mcol = Mt[:, qb + 32 * h:qb + 32 * h + 1]
                for kg in range(nkg):
                    c0 = kg * 512
                    c1 = min((qb + 1) * 128, c0 + 512)
                    W = c1 - c0
                    bank = pbank[2 * h + kg % 2]
                    wts = []
                    if qb == 0 and kg == 0 and h == 0:
                        wts += [(sp, q_done), (vp, v_vsr), (gd, 0)]
                    key = (h, kg % 2)
                    if key in last_p1:
                        wts.append((vp, last_p1[key]))
                    mm = p.op("tensor", lambda e, qsl=qsl, h=h, c0=c0, c1=c1, bank=bank, W=W: e.matmul(
                        bank[:, 0:W], qsl, KA[h][0:64, c0:c1],
                        start=True, stop=True), waits=wts, inc=tn)
                    is_diag = (c1 == (qb + 1) * 128)
                    if is_diag:
                        p.op("vector", lambda e, bank=bank, W=W: e.tensor_mul(
                            bank[:, W - 128:W], bank[:, W - 128:W], tri_f),
                            waits=[(tn, mm)], inc=vp)
                    vv = p.op("vector", lambda e, bank=bank, W=W, kg=kg: e.reduce_max(
                        mtcols[:, kg:kg + 1], bank[:, 0:W], axis=AX.X),
                        waits=[(tn, mm)], inc=vp)
                    last_p1[(h, kg % 2)] = vv
                p.op("vector", lambda e, mcol=mcol, nkg=nkg: e.reduce_max(
                    mcol, mtcols[:, 0:nkg], axis=AX.X), inc=vp)
                p.op("vector", lambda e, mcol=mcol: e.tensor_scalar_max(mcol, mcol, 0.0), inc=vp)
        v_m = vp.n

        p.op("vector", lambda e: e.tensor_scalar(negMtb, Mt, -1.0, None, ALU.mult), inc=vp)
        p.op("vector", lambda e: e.tensor_copy(negMtf, negMtb), inc=vp)
        v_nm = vp.n
        s_em = p.op("scalar", lambda e: e.activation(EMtf, negMtb, AF.Exp, scale=QSCALE),
                    waits=[(vp, v_nm)], inc=sp)
        t_mt = p.op("tensor", lambda e: e.transpose(pbank[5][:, 0:128], stackf, ident_f),
                    waits=[(vp, v_nm), (sp, s_em), (vp, av_free[1])], inc=tn)
        ev_smt = p.op("vector", lambda e: e.tensor_copy(SMt, pbank[5][:, 0:128]),
                      waits=[(tn, t_mt)], inc=vp)
        av_free[1] = ev_smt
        for (ar0, ar1, adst) in [(0, 32, QA[0][64:65, 0:T]), (32, 64, QA[1][64:65, 0:T]),
                                 (64, 96, EMR[0]), (96, 128, EMR[1])]:
            p.op("gpsimd", lambda e, ar0=ar0, ar1=ar1, adst=adst: e.dma_start(
                out=adst.rearrange("p (t c) -> p t c", t=32), in_=SMt[ar0:ar1, :]),
                waits=[(vp, ev_smt), (tn, prev_hg_tn)], inc=gd, inc_k=16)
        g_aug = gd.n

        pendG = None

        def g_tail(gt, avb):
            nonlocal b6_free, b7_free
            for qi in range(4):
                qb = 4 * gt + qi
                if qb > 30:
                    continue
                w0 = [(gd, g_vsr)] if (gt == 0 and qi == 0) else []
                p.op("tensor", lambda e, qb=qb, qi=qi, avb=avb: e.matmul(
                    avb[0:64, qi * 128:(qi + 1) * 128],
                    VSufR[0][0:1, (qb + 1) * 64:(qb + 2) * 64],
                    EMR[0][0:1, qb * 128:(qb + 1) * 128],
                    start=False, stop=False, skip_group_check=True), waits=w0, inc=tn)
                p.op("tensor", lambda e, qb=qb, qi=qi, avb=avb: e.matmul(
                    avb[64:128, qi * 128:(qi + 1) * 128],
                    VSufR[1][0:1, (qb + 1) * 64:(qb + 2) * 64],
                    EMR[1][0:1, qb * 128:(qb + 1) * 128],
                    start=False, stop=False, skip_group_check=True,
                    tile_position=(64, 64)), inc=tn)
            p.op("tensor", lambda e, avb=avb: e.matmul(
                avb[:, 0:512], onesrb[0:1, 0:128], zrow512b,
                start=False, stop=True, skip_group_check=True), inc=tn)
            p.op("tensor", lambda e: e.matmul(
                pbank[6][:, 0:512], onesrb[0:1, 0:128], zrow512b,
                start=False, stop=True, skip_group_check=True), inc=tn)
            t_close = tn.n
            first_z = True
            for h in range(2):
                for qi in range(4):
                    qb = 4 * gt + qi
                    cnt = float(T - (qb + 1) * 128)
                    w = [(tn, t_close)] if first_z else []
                    first_z = False
                    p.op("vector", lambda e, h=h, qi=qi, qb=qb, cnt=cnt: e.scalar_tensor_tensor(
                        Zfin[h][0:1, qi * 128:(qi + 1) * 128],
                        EMR[h][0:1, qb * 128:(qb + 1) * 128], cnt,
                        pbank[6][64 * h:64 * h + 1, qi * 128:(qi + 1) * 128],
                        ALU.mult, ALU.add), waits=w, inc=vp)
                p.op("vector", lambda e, h=h: e.reciprocal(Zfin[h], Zfin[h]), inc=vp)
            v_zr = vp.n
            b6_free = v_zr
            p.op("tensor", lambda e: e.matmul(
                pbank[7][:, 0:512], onesrb[0:1, 0:128], zrow512b, start=True, stop=False,
                skip_group_check=True), waits=[(vp, b7_free)], inc=tn)
            p.op("tensor", lambda e: e.matmul(
                pbank[7][0:64, 0:512], ones_row[0:1, 0:64], Zfin[0],
                start=False, stop=False, skip_group_check=True), waits=[(vp, v_zr)], inc=tn)
            p.op("tensor", lambda e: e.matmul(
                pbank[7][64:128, 0:512], ones64f[0:1, 0:64], Zfin[1],
                start=False, stop=False, skip_group_check=True, tile_position=(64, 64)), inc=tn)
            p.op("tensor", lambda e: e.matmul(
                pbank[7][:, 0:512], onesrb[0:1, 0:128], zrow512b,
                start=False, stop=True, skip_group_check=True), inc=tn)
            t_zbc = tn.n
            ev_zbc = p.op("vector", lambda e: e.tensor_copy(zbcs, pbank[7][:, 0:512]),
                          waits=[(tn, t_zbc)], inc=vp)
            b7_free = ev_zbc
            p.op("vector", lambda e, gt=gt, hg=hg, avb=avb: e.tensor_mul(
                ctx_c(hg)[0:64, gt * 512:(gt + 1) * 512], avb[0:64, 0:512],
                zbcs[0:64, :]), waits=[(tn, t_close)], inc=vp)
            p.op("vector", lambda e, gt=gt, hg=hg, avb=avb: e.tensor_mul(
                ctx_c(hg)[64:128, gt * 512:(gt + 1) * 512], avb[64:128, 0:512],
                zbcs[64:128, :]), inc=vp)
            av_free[gt % 2] = vp.n

        for g in range(8):
            kmax = 4 * g + 3
            avb = pbank[4 + g % 2]

            def emit_avz(kb2, c02, pks, avb=avb):
                for h2 in range(2):
                    pk2 = pks[h2]
                    avw = [(sp, pk2[1])]
                    if h2 == 0:
                        p.op("tensor", lambda e, kb2=kb2, c02=c02, pk2=pk2, avb=avb: e.matmul(
                            avb[0:64, c02:512], VTA[:, kb2 * 128:kb2 * 128 + 64],
                            pk2[0][:, c02:512], start=False, stop=False, skip_group_check=True),
                            waits=avw, inc=tn)
                    else:
                        p.op("tensor", lambda e, kb2=kb2, c02=c02, pk2=pk2, avb=avb: e.matmul(
                            avb[64:128, c02:512], VTA[:, kb2 * 128 + 64:(kb2 + 1) * 128],
                            pk2[0][:, c02:512], start=False, stop=False, skip_group_check=True,
                            tile_position=(0, 64)), waits=avw, inc=tn)
                    p.op("tensor", lambda e, h2=h2, c02=c02, pk2=pk2: e.matmul(
                        pbank[6][64 * h2:64 * h2 + 1, c02:512], ones_col_b, pk2[0][:, c02:512],
                        start=False, stop=False, skip_group_check=True,
                        tile_position=(0, 64 * h2)), inc=tn)
                    last_av[(h2, kb2 % 2)] = tn.n

            pend = None
            for kb in range(kmax + 1):
                j = kb - 4 * g
                c0 = max(j, 0) * 128
                par = kb % 2
                mms = {}
                for h in range(2):
                    sbank = pbank[2 * h + par]
                    wts = []
                    key = (h, par)
                    if key in last_exp:
                        wts.append((sp, last_exp[key]))
                    if g == 0 and kb == 0 and h == 0:
                        wts.append((gd, g_aug))
                    if j < 0:
                        mms[h] = p.op("tensor", lambda e, h=h, kb=kb, c0=c0, g=g, sbank=sbank: e.matmul(
                            sbank[:, c0:512], KA[h][0:65, kb * 128:(kb + 1) * 128],
                            QA[h][0:65, g * 512 + c0:(g + 1) * 512], start=True, stop=True),
                            waits=wts, inc=tn)
                    else:
                        mms[h] = p.op("tensor", lambda e, h=h, kb=kb, c0=c0, g=g, sbank=sbank: e.matmul(
                            sbank[:, c0:512], KA[h][0:64, kb * 128:(kb + 1) * 128],
                            QA[h][0:64, g * 512 + c0:(g + 1) * 512], start=True, stop=False,
                            skip_group_check=True), waits=wts, inc=tn)
                if j >= 0:
                    vms = {}
                    for h in range(2):
                        sbank = pbank[2 * h + par]
                        vms[h] = p.op("vector", lambda e, sbank=sbank, c0=c0: e.tensor_mul(
                            sbank[:, c0:c0 + 128], sbank[:, c0:c0 + 128], triT_f),
                            waits=[(tn, mms[h])], inc=vp)
                    for h in range(2):
                        sbank = pbank[2 * h + par]
                        mms[h] = p.op("tensor", lambda e, h=h, kb=kb, c0=c0, g=g, sbank=sbank: e.matmul(
                            sbank[:, c0:512], KA[h][64:65, kb * 128:(kb + 1) * 128],
                            QA[h][64:65, g * 512 + c0:(g + 1) * 512], start=False, stop=True,
                            skip_group_check=True, tile_position=(64, 0)),
                            waits=[(vp, vms[h])], inc=tn)
                pks = {}
                for h in range(2):
                    sbank = pbank[2 * h + par]
                    pk = PBK[h][par]
                    wts2 = [(tn, mms[h])]
                    if (h, par) in last_av:
                        wts2.append((tn, last_av[(h, par)]))
                    ex = p.op("scalar", lambda e, pk=pk, sbank=sbank, c0=c0: e.activation(
                        pk[:, c0:512], sbank[:, c0:512], AF.Exp, scale=QSCALE),
                        waits=wts2, inc=sp)
                    last_exp[(h, par)] = ex
                    pks[h] = (pk, ex)
                if kb == 0:
                    if pendG is not None:
                        g_tail(*pendG)
                        pendG = None
                    p.op("tensor", lambda e, avb=avb: e.matmul(
                        avb[:, 0:512], onesrb[0:1, 0:128], zrow512b, start=True, stop=False,
                        skip_group_check=True), waits=[(vp, av_free[g % 2])], inc=tn)
                    p.op("tensor", lambda e: e.matmul(
                        pbank[6][:, 0:512], onesrb[0:1, 0:128], zrow512b, start=True, stop=False,
                        skip_group_check=True), waits=[(vp, b6_free)], inc=tn)
                if pend is not None:
                    emit_avz(*pend)
                pend = (kb, c0, pks)
            emit_avz(*pend)
            pendG = (g, avb)
        g_tail(*pendG)
        pendG = None
        prev_q_done = q_done
        prev_v_done = v_done
        prev_hg_exp = sp.n
        prev_hg_tn = tn.n
    v_ctx_done = vp.n

    if debug >= 2:
        for c in range(8):
            p.op("gpsimd", lambda e, c=c: e.dma_start(out=dbg["ctx"][c], in_=ctx_c(c)),
                 waits=[(vp, v_ctx_done)], inc=sod, inc_k=16)
        p.wait("vector", sod, sod.n)
    if debug >= 5:
        sodB = Ch(nc, "sodB")
        p.dma_chs.add(sodB)
        for nm, ap_ in [("KQ", KQ[:, 0:8192]), ("VT", VT[:, 0:4096]),
                        ("PT", PT[:, 0:8448]), ("SM", SM[:, 0:3072]),
                        ("XS", XS[:, 0:4096])]:
            dt_ = F32 if nm in ("SM", "XS") else BF16
            w_ = ap_.shape[-1]
            dbg[nm] = nc.dram_tensor("dbgB_" + nm, [128, w_], dt_, kind="ExternalOutput").ap()
            p.op("gpsimd", lambda e, nm=nm, ap_=ap_: e.dma_start(out=dbg[nm], in_=ap_),
                 waits=[(vp, v_ctx_done), (tn, tn.n), (sp, sp.n)], inc=sodB, inc_k=16)
        p.wait("vector", sodB, sodB.n)
        p.wait("tensor", sodB, sodB.n)
        p.wait("scalar", sodB, sodB.n)
        p.wait("sync", sodB, sodB.n)
        p.wait("gpsimd", sodB, sodB.n)

    p.mark("B_attn0")
    wou_sl = [WS[:, 0:1024].rearrange("p (c e) -> p c e", c=8),
              WS[:, 1024:2048].rearrange("p (c e) -> p c e", c=8)]
    vtv = VT[:].rearrange("p (a t) -> p a t", a=8)
    last_cp = vp.n
    last_wo_used = {0: 0, 1: 0}
    for tg in range(NTG):
        for ech in range(8):
            bank = pbank[ech % 2]
            wsl_o = wou_sl[ech % 2]
            gch = gdA if ech % 2 == 0 else gdB
            p.op("gpsimd", lambda e, ech=ech, wsl_o=wsl_o: e.dma_start(
                out=wsl_o, in_=w_outT.rearrange("(c p) e -> p c e", p=128)[:, :, ech * 128:(ech + 1) * 128]),
                waits=[(sp, prev_q_done if (tg == 0 and ech < 2) else 0),
                       (tn, last_wo_used[ech % 2])], inc=gch, inc_k=16)
            for c in range(8):
                wts = []
                if c == 0:
                    wts = [(gch, 0)]
                    if tg == 0 and ech == 0:
                        wts.append((vp, v_ctx_done))
                    elif tg > 0 or ech >= 2:
                        wts.append((sp, sp.n - 1))
                p.op("tensor", lambda e, c=c, ech=ech, tg=tg, bank=bank, wsl_o=wsl_o: e.matmul(
                    bank[:, 0:TG], wsl_o[:, c, :],
                    ctx_c(c)[:, tg * TG:(tg + 1) * TG],
                    start=(c == 0), stop=(c == 7)), waits=wts, inc=tn)
            last_wo_used[ech % 2] = tn.n
            t_mm = tn.n
            p.op("scalar", lambda e, ech=ech, bank=bank: e.copy(
                vtv[:, ech, :], bank[:, 0:TG]), waits=[(tn, t_mm)] + ([(vp, last_cp)] if ech == 0 else []), inc=sp)
        s_ev = sp.n
        t_all = tn.n
        for ech in range(8):
            last_cp = p.op("vector", lambda e, ech=ech, tg=tg: e.tensor_copy(
                B[:, ech * T + tg * TG: ech * T + (tg + 1) * TG], vtv[:, ech, :]),
                waits=[(sp, s_ev), (tn, t_all)] if ech == 0 else ())
    v_s0_done = vp.n
    p.wait("tensor", vp, v_s0_done)

    p.mark("C0_outproj")
    wg_b = nc.dram_tensor("wg_b", [D, HIDDEN], BF16, kind="Internal").ap()
    wu_b = nc.dram_tensor("wu_b", [D, HIDDEN], BF16, kind="Internal").ap()
    wd_b = nc.dram_tensor("wd_b", [HIDDEN, D], BF16, kind="Internal").ap()
    cst0 = p.op("gpsimd", lambda e: e.dma_start(out=wg_b, in_=wgT), inc=gd, inc_k=16)
    cst1 = p.op("gpsimd", lambda e: e.dma_start(out=wu_b, in_=wuT), inc=gd, inc_k=16)
    cst2 = p.op("gpsimd", lambda e: e.dma_start(out=wd_b, in_=wdT), inc=gd, inc_k=16)
    wsl = [PT[:, 4096:6144].rearrange("p (w c e) -> p w c e", w=2, c=8),
           PT[:, 6144:8192].rearrange("p (w c e) -> p w c e", w=2, c=8)]
    wdbuf = [WS[:, 0:1024], WS[:, 1024:2048]]
    XSB = XS.bitcast(BF16)
    _ht = [XSB[:, i * TG:(i + 1) * TG] for i in range(16)] + \
          [PT[:, i * TG:(i + 1) * TG] for i in range(8)]
    def htile_s(hc):
        return _ht[hc]
    htmp = SM.bitcast(BF16)[:, 4096:4608]

    last_slot_consumed = {0: tn.n, 1: tn.n}
    t_down_prev = 0
    htmp2 = [htmp, SM.bitcast(BF16)[:, 4608:5120]]
    for tg in range(NTG):
        silu_by_par = {}
        mul_by_par = {}
        for hc in range(24):
            slot = hc % 2
            wv = wsl[slot]
            gch = gdA if slot == 0 else gdB
            bA = pbank[2 + 2 * (hc % 2)]
            bB = pbank[3 + 2 * (hc % 2)]
            htm = htmp2[hc % 2]
            p.op("gpsimd", lambda e, hc=hc, wv=wv: e.dma_start(
                out=wv[:, 0], in_=wg_b.rearrange("(c p) e -> p c e", p=128)[:, :, hc * 128:(hc + 1) * 128]),
                waits=[(tn, last_slot_consumed[slot]), (gd, 0)], inc=gch, inc_k=16)
            p.op("gpsimd", lambda e, hc=hc, wv=wv: e.dma_start(
                out=wv[:, 1], in_=wu_b.rearrange("(c p) e -> p c e", p=128)[:, :, hc * 128:(hc + 1) * 128]),
                inc=gch, inc_k=16)
            for c in range(8):
                wts = []
                if c == 0:
                    wts = [(gch, 0)]
                    if hc % 2 in silu_by_par:
                        wts.append((sp, silu_by_par[hc % 2]))
                    else:
                        wts.append((sp, sp.n))
                    if tg == 0 and hc == 0:
                        wts.append((vp, v_s0_done))
                p.op("tensor", lambda e, c=c, tg=tg, wv=wv, bA=bA: e.matmul(
                    bA[:, 0:TG], wv[:, 0, c, :], xn_c(c)[:, tg * TG:(tg + 1) * TG],
                    start=(c == 0), stop=(c == 7)), waits=wts, inc=tn)
            t_gate = tn.n
            for c in range(8):
                wts = []
                if c == 0:
                    wts = [(vp, mul_by_par[hc % 2])] if hc % 2 in mul_by_par else [(vp, vp.n)]
                p.op("tensor", lambda e, c=c, tg=tg, wv=wv, bB=bB: e.matmul(
                    bB[:, 0:TG], wv[:, 1, c, :], xn_c(c)[:, tg * TG:(tg + 1) * TG],
                    start=(c == 0), stop=(c == 7)), waits=wts, inc=tn)
            t_gu = tn.n
            sgt = p.op("scalar", lambda e, bA=bA, htm=htm: e.activation(
                htm, bA[:, 0:TG], AF.Silu, scale=DEQ),
                waits=[(tn, t_gate)] + ([(vp, mul_by_par[hc % 2])] if hc % 2 in mul_by_par else []),
                inc=sp)
            silu_by_par[hc % 2] = sgt
            mul_by_par[hc % 2] = p.op("vector", lambda e, hc=hc, bB=bB, htm=htm: e.tensor_mul(
                htile_s(hc), htm, bB[:, 0:TG]),
                waits=[(sp, sgt), (tn, t_gu), (tn, t_down_prev)], inc=vp)
            last_slot_consumed[slot] = t_gu
        v_h = vp.n
        for hc in range(24):
            slot = hc % 2
            wdb = wdbuf[slot]
            gch = gdA if slot == 0 else gdB
            p.op("gpsimd", lambda e, hc=hc, wdb=wdb: e.dma_start(
                out=wdb, in_=wd_b[hc * 128:(hc + 1) * 128, :]),
                waits=[(tn, last_slot_consumed[slot]), (gd, 0)], inc=gch, inc_k=16)
            for ech in range(8):
                wts = []
                if ech == 0:
                    wts = [(gch, 0), (vp, v_h)]
                    if hc == 0:
                        wts.append((sp, sp.n))
                p.op("tensor", lambda e, hc=hc, ech=ech, wdb=wdb: e.matmul(
                    pbank[ech][:, 0:TG], wdb[:, ech * 128:(ech + 1) * 128],
                    htile_s(hc), start=(hc == 0), stop=(hc == 23)),
                    waits=wts, inc=tn)
            last_slot_consumed[slot] = tn.n
        t_down_prev = tn.n
        for ech in range(8):
            p.op("scalar", lambda e, ech=ech, tg=tg: e.activation(
                xn_c(ech)[:, tg * TG:(tg + 1) * TG], pbank[ech][:, 0:TG],
                AF.Silu, scale=DEQ * DEQ), waits=[(tn, t_down_prev)] if ech == 0 else (), inc=sp)
    s_g0_done = sp.n

    p.mark("C1_ffn")
    vtx = VT[:].rearrange("p (c t) -> p c t", c=8)
    wdkv = WS[:, 0:2048].rearrange("p (c l) -> p c l", c=8)
    ldkv = p.op("gpsimd", lambda e: e.dma_start(
        out=wdkv, in_=WdkvT.rearrange("(c p) l -> p c l", p=128)),
        waits=[(tn, t_down_prev)], inc=gd, inc_k=16)
    tmpC = SM[:, 0:512]
    sqC = SM[:, 512:1024]

    def ln_chunk(xview, dst_view, first_waits, ones_x=None, r3x=None):
        oc = ones_col if ones_x is None else ones_x
        r3l = r3 if r3x is None else r3x
        for c in range(8):
            p.op("tensor", lambda e, c=c, xview=xview, oc=oc: e.matmul(
                pbank[6][0:1, 0:TG], oc, xview[:, c, :], start=(c == 0), stop=(c == 7)),
                waits=first_waits if c == 0 else (), inc=tn)
        t_sum2 = tn.n
        smm = {}
        for c in range(8):
            sv = p.op("scalar", lambda e, c=c, xview=xview: e.square(sqC, xview[:, c, :]),
                      waits=[(tn, smm[c - 1])] if c >= 1 else [(tn, t_sum2)], inc=sp)
            smm[c] = p.op("tensor", lambda e, c=c: e.matmul(
                pbank[7][0:1, 0:TG], ones_col, sqC, start=(c == 0), stop=(c == 7)),
                waits=[(sp, sv)], inc=tn)
        t_sq2 = tn.n
        p.op("vector", lambda e: e.tensor_scalar(r1, pbank[6][0:1, 0:TG], 1.0 / D, None, ALU.mult),
             waits=[(tn, t_sq2)])
        p.op("vector", lambda e: e.tensor_scalar(r2, pbank[7][0:1, 0:TG], 1.0 / D, None, ALU.mult))
        p.op("vector", lambda e: e.tensor_tensor(r3l, r1, r1, ALU.mult))
        p.op("vector", lambda e: e.tensor_tensor(r2, r2, r3l, ALU.subtract))
        p.op("vector", lambda e: e.tensor_scalar(r2, r2, EPS_LN, None, ALU.add))
        v_st = vp.n
        p.op("scalar", lambda e: e.sqrt(r2, r2), waits=[(vp, v_st)])
        s_sq = sp.n
        p.op("vector", lambda e: e.reciprocal(r2, r2), waits=[(sp, s_sq)])
        p.op("vector", lambda e: e.tensor_tensor(r1, r1, r2, ALU.mult))
        p.op("vector", lambda e: e.tensor_scalar(r1, r1, -1.0, None, ALU.mult))
        v_ab2 = vp.n
        p.op("tensor", lambda e: e.matmul(pbank[4][:, 0:TG], ones_row, r2, start=True, stop=True),
             waits=[(vp, v_ab2)], inc=tn)
        p.op("tensor", lambda e: e.matmul(pbank[5][:, 0:TG], ones_row, r1, start=True, stop=True),
             inc=tn)
        t_bc2 = tn.n
        for c in range(8):
            p.op("vector", lambda e, c=c, xview=xview: e.tensor_tensor(
                tmpC, xview[:, c, :], pbank[4][:, 0:TG], ALU.mult),
                waits=[(tn, t_bc2)] if c == 0 else ())
            p.op("vector", lambda e, c=c, dst_view=dst_view: e.tensor_tensor(
                dst_view[:, c, :], tmpC, pbank[5][:, 0:TG], ALU.add))
        return vp.n

    for tg in range(NTG):
        xsv = XS[:].rearrange("p (c t) -> p c t", c=8)
        src = xT.rearrange("(c p) t -> p c t", p=128)[:, :, tg * TG:(tg + 1) * TG]
        wts = [(sp, s_g0_done)] if tg == 0 else [(tn, tn.n)]
        ldx = p.op("sync", lambda e, s=src, d=xsv: e.dma_start(out=d, in_=s),
                   waits=wts + [(vp, vp.n)], inc=ld, inc_k=16)
        p.wait("vector", ld, ldx)
        for c in range(8):
            p.op("vector", lambda e, c=c, tg=tg: e.tensor_mul(
                tmpC, B[:, c * T + tg * TG: c * T + (tg + 1) * TG],
                xn_c(c)[:, tg * TG:(tg + 1) * TG]))
            p.op("vector", lambda e, c=c, xsv=xsv: e.tensor_add(
                xsv[:, c, :], xsv[:, c, :], tmpC))
        v_x1 = vp.n
        ln_chunk(xsv, vtx, [(vp, v_x1)])
        v_xn1 = vp.n
        for sub in range(4):
            bank = pbank[2 + sub % 2]
            for c in range(8):
                p.op("tensor", lambda e, c=c, sub=sub, bank=bank: e.matmul(
                    bank[:, 0:LAT], vtx[:, c, sub * 128:(sub + 1) * 128], wdkv[:, c, :],
                    start=(c == 0), stop=(c == 7)),
                    waits=[(vp, v_xn1), (gd, ldkv), (sp, sp.n)] if c == 0 else (), inc=tn)
            t_ck = tn.n
            p.op("scalar", lambda e, tg=tg, sub=sub, bank=bank: e.copy(
                PT[:, (tg * 4 + sub) * LAT:(tg * 4 + sub + 1) * LAT], bank[:, 0:LAT]),
                waits=[(tn, t_ck)], inc=sp)
        xsv4 = XS[:].rearrange("p (c b q) -> p c b q", c=8, b=4)
        for c in range(8):
            p.op("vector", lambda e, c=c, xsv4=xsv4: e.tensor_tensor(
                tmpC[:, 0:256].rearrange("p (b q) -> p b q", b=2),
                xsv4[:, c, 1:4:2, :], xsv4[:, c, 0:4:2, :], ALU.subtract))
            p.op("vector", lambda e, c=c, tg=tg, xsv4=xsv4: e.scalar_tensor_tensor(
                A[:, c * T + tg * TG: c * T + tg * TG + 256].rearrange("p (b q) -> p b q", b=2),
                tmpC[:, 0:256].rearrange("p (b q) -> p b q", b=2),
                u_col[:, 0:1], xsv4[:, c, 0:4:2, :], ALU.mult, ALU.add))
    s_ckv_done = sp.n
    for c in range(8):
        for tg in range(NTG):
            p.op("vector", lambda e, c=c, tg=tg: e.tensor_copy(
                A[:, c * 2048 + tg * 256: c * 2048 + (tg + 1) * 256],
                A[:, c * T + tg * TG: c * T + tg * TG + 256]))
    v_c2_done = vp.n

    if debug >= 3:
        for c in range(8):
            p.op("gpsimd", lambda e, c=c: e.dma_start(
                out=dbg["x1o"][c], in_=A[:, c * 2048:(c + 1) * 2048]),
                waits=[(vp, v_c2_done), (sp, s_ckv_done)], inc=sod, inc_k=16)
        p.op("gpsimd", lambda e: e.dma_start(
            out=dbg["ckv"].rearrange("t p l -> p t l"),
            in_=PT[:, 0:32 * LAT].rearrange("p (t l) -> p t l", t=32)),
            waits=[(sp, s_ckv_done)], inc=sod, inc_k=16)
        p.wait("vector", sod, sod.n)
        p.wait("sync", sod, sod.n)
        p.wait("tensor", sod, sod.n)

    p.mark("C2_epi")
    for t_ in range(32):
        for l_ in range(2):
            p.op("sync", lambda e, t_=t_, l_=l_: e.dma_start_transpose(
                KQ[:, l_ * T + t_ * 128: l_ * T + (t_ + 1) * 128],
                PT[:, t_ * LAT + l_ * 128: t_ * LAT + (l_ + 1) * 128]),
                waits=[(sp, s_ckv_done), (vp, v_c2_done), (tn, tn.n)] if (t_ == 0 and l_ == 0) else (),
                inc=tsp, inc_k=16)
    t_ckvT = tsp.n

    def ckvT_c(l_):
        return KQ[:, l_ * T:(l_ + 1) * T]

    vt_wuv = VT[:, 0:2048].rearrange("p (c l) -> p c l", c=8)
    vt_wuk = VT[:, 2048:4096].rearrange("p (c l) -> p c l", c=8)
    ws_wuq = WS[:, 0:2048].rearrange("p (c l) -> p c l", c=8)
    l1 = p.op("gpsimd", lambda e: e.dma_start(
        out=vt_wuv, in_=Wuv.rearrange("(c p) l -> p c l", p=128)),
        waits=[(vp, v_c2_done), (sp, s_ckv_done)], inc=gd, inc_k=16)
    l2 = p.op("gpsimd", lambda e: e.dma_start(
        out=vt_wuk, in_=Wuk.rearrange("(c p) l -> p c l", p=128)), inc=gd, inc_k=16)
    l3 = p.op("gpsimd", lambda e: e.dma_start(
        out=ws_wuq, in_=Wuq.rearrange("(c p) l -> p c l", p=128)),
        waits=[(sp, s_ckv_done)], inc=gd, inc_k=16)
    for am in range(2):
        for c in range(8):
            p.op("tensor", lambda e, am=am, c=c: e.matmul(
                pbank[am][:, 0:LAT], ws_wuq[:, c, am * 128:(am + 1) * 128], vt_wuk[:, c, :],
                start=(c == 0), stop=(c == 7)),
                waits=[(gd, l3), (sp, sp.n)] if (am == 0 and c == 0) else (), inc=tn)
    t_inner = tn.n
    ws_inner = WS[:, 0:512].rearrange("p (a l) -> p a l", a=2)
    for am in range(2):
        p.op("scalar", lambda e, am=am: e.copy(ws_inner[:, am, :], pbank[am][:, 0:LAT]),
             waits=[(tn, t_inner)] if am == 0 else (), inc=sp)
    s_inner = sp.n
    vt_wo = VT[:, 2048:4096].rearrange("p (c n) -> p c n", c=8)
    FA = A[:, 18432:20480].rearrange("p (l d) -> p l d", l=2)
    t_prev_f = t_inner
    for dg in range(4):
        lwo = p.op("gpsimd", lambda e, dg=dg: e.dma_start(
            out=vt_wo, in_=WoT.rearrange("(c p) d -> p c d", p=128)[:, :, dg * 256:(dg + 1) * 256]),
            waits=[(tn, t_prev_f)], inc=gdA, inc_k=16)
        for lm in range(2):
            for c in range(8):
                p.op("tensor", lambda e, lm=lm, c=c, dg=dg: e.matmul(
                    pbank[2 + lm][:, 0:256], vt_wuv[:, c, lm * 128:(lm + 1) * 128], vt_wo[:, c, :],
                    start=(c == 0), stop=(c == 7)),
                    waits=[(gdA, 0), (sp, sp.n)] if (lm == 0 and c == 0) else (), inc=tn)
        t_f = tn.n
        t_prev_f = t_f
        for lm in range(2):
            p.op("scalar", lambda e, lm=lm, dg=dg: e.copy(
                FA[:, lm, dg * 256:(dg + 1) * 256], pbank[2 + lm][:, 0:256]),
                waits=[(tn, t_f)] if lm == 0 else (), inc=sp)
    s_F = sp.n
    vt_wdq = VT[:, 0:2048].rearrange("p (a v) -> p a v", a=2)
    lwdq = p.op("gpsimd", lambda e: e.dma_start(
        out=vt_wdq, in_=Wdq.rearrange("(a p) v -> p a v", p=128)),
        waits=[(tn, t_f)], inc=gd, inc_k=16)
    ETA = A[:, 16384:18432].rearrange("p (l v) -> p l v", l=2)
    for lm in range(2):
        for ng in range(2):
            for ac in range(2):
                p.op("tensor", lambda e, lm=lm, ng=ng, ac=ac: e.matmul(
                    pbank[4 + lm][:, 0:TG], ws_inner[:, ac, lm * 128:(lm + 1) * 128],
                    vt_wdq[:, ac, ng * TG:(ng + 1) * TG],
                    start=(ac == 0), stop=(ac == 1)),
                    waits=([(gd, lwdq), (sp, s_F)] if (lm == 0 and ng == 0 and ac == 0) else
                           ([(sp, sp.n)] if ac == 0 else [])), inc=tn)
            t_et = tn.n
            p.op("scalar", lambda e, lm=lm, ng=ng: e.copy(
                ETA[:, lm, ng * TG:(ng + 1) * TG], pbank[4 + lm][:, 0:TG]),
                waits=[(tn, t_et)], inc=sp)
    s_ET = sp.n

    ws_wdqT = WS[:, 0:2048].rearrange("p (c a) -> p c a", c=8)
    lq1 = p.op("gpsimd", lambda e: e.dma_start(
        out=ws_wdqT, in_=WdqT.rearrange("(c p) a -> p c a", p=128)),
        waits=[(sp, s_ET), (tn, tn.n)], inc=gd, inc_k=16)
    q1A = A[:, 20480:24576].rearrange("p (a t) -> p a t", a=2)
    kqv = A[:, 0:16384].rearrange("p (c t) -> p c t", c=8)
    for tgo in range(4):
        x1v = kqv[:, :, tgo * TG:(tgo + 1) * TG]
        ln_chunk(x1v, vtx, [(vp, vp.n), (sp, sp.n)], ones_x=ones_col_b)
        v_xo = vp.n
        for am in range(2):
            bank = pbank[2 + am]
            for c in range(8):
                p.op("tensor", lambda e, am=am, c=c, bank=bank: e.matmul(
                    bank[:, 0:TG], ws_wdqT[:, c, am * 128:(am + 1) * 128], vtx[:, c, :].rearrange("p a t -> p (a t)") if False else vtx[:, c, :],
                    start=(c == 0), stop=(c == 7)),
                    waits=[(gd, lq1), (vp, v_xo), (sp, sp.n)] if c == 0 else (), inc=tn)
            t_q1 = tn.n
            p.op("scalar", lambda e, am=am, tgo=tgo, bank=bank: e.copy(
                q1A[:, am, tgo * TG:(tgo + 1) * TG], bank[:, 0:TG]),
                waits=[(tn, t_q1)], inc=sp)
    s_q1 = sp.n
    ws_wuqT = WS[:, 0:2048].rearrange("p (a v) -> p a v", a=2)
    lq2 = p.op("gpsimd", lambda e: e.dma_start(
        out=ws_wuqT, in_=WuqT.rearrange("(a p) v -> p a v", p=128)),
        waits=[(tn, tn.n)], inc=gd, inc_k=16)
    for tgo in range(4):
        for vm in range(8):
            bank = pbank[vm % 2]
            for ac in range(2):
                p.op("tensor", lambda e, vm=vm, ac=ac, tgo=tgo, bank=bank: e.matmul(
                    bank[:, 0:TG], ws_wuqT[:, ac, vm * 128:(vm + 1) * 128],
                    q1A[:, ac, tgo * TG:(tgo + 1) * TG],
                    start=(ac == 0), stop=(ac == 1)),
                    waits=[(gd, lq2), (sp, s_q1)] if (tgo == 0 and vm == 0 and ac == 0) else
                          ([(sp, sp.n - 1)] if (ac == 0 and (vm >= 2 or tgo > 0)) else []), inc=tn)
            t_q = tn.n
            p.op("scalar", lambda e, vm=vm, tgo=tgo, bank=bank: e.copy(
                B[:, vm * 2048 + tgo * TG: vm * 2048 + (tgo + 1) * TG], bank[:, 0:TG]),
                waits=[(tn, t_q)], inc=sp)
    s_q = sp.n

    if debug >= 3:
        for c in range(8):
            p.op("gpsimd", lambda e, c=c: e.dma_start(
                out=dbg["q"][c], in_=B[:, c * 2048:(c + 1) * 2048]),
                waits=[(sp, s_q)], inc=sod, inc_k=16)
        p.wait("vector", sod, sod.n)
        p.wait("tensor", sod, sod.n)

    K1A = A[:, 20480:24576]
    ZtL = SM[:, 192:256]
    ZrL = SM[:, 384:448]
    zpL = [SM[:, 448:449], SM[:, 449:450]]
    zrow512 = SM[0:1, 2176:2688]
    ctxs = SM.bitcast(BF16)[:, 5376:5888]
    zrow_m = p.op("vector", lambda e: e.memset(zrow512, 0.0), waits=[(vp, vp.n)])
    last_exp = {}
    last_zadd2 = {}
    prev_j_done = vp.n
    for hgl in range(8):
        for kgp in range(8):
            for lm in range(2):
                p.op("tensor", lambda e, kgp=kgp, lm=lm, hgl=hgl: e.matmul(
                    pbank[6 + kgp % 2][:, 0:TG], ETA[:, lm, hgl * 128:(hgl + 1) * 128],
                    ckvT_c(lm)[:, kgp * TG:(kgp + 1) * TG],
                    start=(lm == 0), stop=(lm == 1)),
                    waits=([(tsp, t_ckvT), (sp, s_q)] if (kgp == 0 and lm == 0 and hgl == 0) else
                           ([(sp, sp.n - 1), (vp, prev_j_done)] if (lm == 0 and (kgp >= 2 or hgl > 0)) else
                            ([(vp, prev_j_done)] if lm == 0 else []))), inc=tn)
            t_k1 = tn.n
            p.op("scalar", lambda e, kgp=kgp: e.copy(
                K1A[:, kgp * TG:(kgp + 1) * TG], pbank[6 + kgp % 2][:, 0:TG]),
                waits=[(tn, t_k1)], inc=sp)
        s_k1 = sp.n
        def d_tail(u):
            nonlocal prev_j_done
            j2, nk2, v_zr2 = u
            p.op("tensor", lambda e: e.matmul(
                pbank[4][:, 0:512], ones_row, zrow512, start=True, stop=False,
                skip_group_check=True),
                waits=[(tspA, tspA.n), (tspB, tspB.n), (vp, prev_j_done)], inc=tn)
            for kb in range(nk2):
                for l_ in range(2):
                    for h in range(2):
                        p.op("tensor", lambda e, kb=kb, l_=l_, h=h: e.matmul(
                            pbank[4][:, (2 * h + l_) * 128:(2 * h + l_ + 1) * 128],
                            PT[:, kb * LAT + l_ * 128: kb * LAT + (l_ + 1) * 128],
                            XSB[:, h * T + kb * 128: h * T + (kb + 1) * 128],
                            start=False, stop=False,
                            skip_group_check=True), inc=tn)
            p.op("tensor", lambda e: e.matmul(
                pbank[4][:, 0:512], ones_row, zrow512, start=False, stop=True,
                skip_group_check=True), inc=tn)
            t_av2 = tn.n
            b5v = prev_j_done
            for h in range(2):
                tz = p.op("tensor", lambda e, j2=j2, h=h: e.transpose(
                    pbank[5][0:1, 0:128], ZrL[:, 2 * j2 + h:2 * j2 + h + 1], ident_f),
                    waits=[(vp, v_zr2), (vp, b5v)], inc=tn)
                ez = p.op("vector", lambda e: e.tensor_copy(zr2[0:1, :], pbank[5][0:1, 0:128]),
                          waits=[(tn, tz)], inc=vp)
                tb = p.op("tensor", lambda e: e.matmul(
                    pbank[5][:, 128:256], ones_row, zr2[0:1, :], start=True, stop=True),
                    waits=[(vp, ez)], inc=tn)
                cz = p.op("vector", lambda e: e.tensor_copy(zbc, pbank[5][:, 128:256]),
                          waits=[(tn, tb)], inc=vp)
                for l_ in range(2):
                    p.op("vector", lambda e, h=h, l_=l_: e.tensor_mul(
                        ctxs[:, (2 * h + l_) * 128:(2 * h + l_ + 1) * 128],
                        pbank[4][:, (2 * h + l_) * 128:(2 * h + l_ + 1) * 128], zbc),
                        waits=[(tn, t_av2)] if l_ == 0 else ())
                b5v = vp.n
            v_ctx2 = vp.n
            p.op("tensor", lambda e: e.matmul(
                pbank[6][:, 0:128], ones_row, zrow512[0:1, 0:128], start=True, stop=False,
                skip_group_check=True), waits=[(vp, v_ctx2)], inc=tn)
            for h in range(2):
                h_glob2 = hgl * 2 + h
                for l_ in range(2):
                    p.op("tensor", lambda e, h=h, l_=l_, h_glob2=h_glob2: e.matmul(
                        pbank[6][64 * h:64 * h + 64, 0:128],
                        FA[:, l_, h_glob2 * 64:(h_glob2 + 1) * 64],
                        ctxs[:, (2 * h + l_) * 128:(2 * h + l_ + 1) * 128],
                        start=False, stop=False,
                        skip_group_check=True), inc=tn)
            p.op("tensor", lambda e: e.matmul(
                pbank[6][:, 0:128], ones_row, zrow512[0:1, 0:128], start=False, stop=True,
                skip_group_check=True), inc=tn)
            t_y = tn.n
            prev_j_done = p.op("vector", lambda e, hgl=hgl, j2=j2: e.tensor_copy(
                B[:, 16384 + hgl * 2048 + j2 * 128: 16384 + hgl * 2048 + (j2 + 1) * 128],
                pbank[6][:, 0:128]), waits=[(tn, t_y)], inc=vp)

        pendD = None
        for j in range(NBO):
            nk = 2 * j + 2
            s_prs = {}
            for h in range(2):
                q_sl = B[64 * h:64 * h + 64, hgl * 2048 + j * 128: hgl * 2048 + (j + 1) * 128]
                zcol = ZtL[:, 2 * j + h:2 * j + h + 1]
                p.op("vector", lambda e, zcol=zcol: e.memset(zcol, 0.0))
                nkg = (nk + 3) // 4
                tspX = tspA if h == 0 else tspB
                for kg in range(nkg):
                    c0 = kg * 512
                    c1 = min(nk * 128, c0 + 512)
                    W = c1 - c0
                    bank = pbank[2 * h + kg % 2]
                    wts = [(sp, s_k1)]
                    key = (h, kg % 2)
                    if key in last_exp:
                        wts.append((sp, last_exp[key]))
                    mm = p.op("tensor", lambda e, q_sl=q_sl, c0=c0, c1=c1, bank=bank, h=h, W=W: e.matmul(
                        bank[:, 0:W], q_sl, K1A[64 * h:64 * h + 64, c0:c1],
                        start=True, stop=True), waits=wts, inc=tn)
                    vmask = None
                    for rel, mk in ((nk - 2, mkA_f), (nk - 1, mkB_f)):
                        boff = rel * 128
                        if c0 <= boff < c1:
                            vmask = p.op("vector", lambda e, bank=bank, boff=boff, c0=c0, mk=mk: e.tensor_add(
                                bank[:, boff - c0:boff - c0 + 128],
                                bank[:, boff - c0:boff - c0 + 128], mk),
                                waits=[(tn, mm)], inc=vp)
                    zp = zpL[kg % 2]
                    wts2 = [(vp, vmask)] if vmask else [(tn, mm)]
                    if kg == 0:
                        wts2.append((tspX, tspX.n))
                    if (kg % 2) in last_zadd2:
                        wts2.append((vp, last_zadd2[kg % 2]))
                    pdst = VT[:, 0:T] if h == 0 else A[:, 24576:24576 + T]
                    sx = p.op("scalar", lambda e, bank=bank, W=W, c0=c0, zp=zp, pdst=pdst: e.activation(
                        pdst[:, c0:c0 + W], bank[:, 0:W], AF.Exp,
                        bias=negml1_col[:, 0:1], scale=QSCALE, accum_out=zp), waits=wts2, inc=sp)
                    last_exp[(h, kg % 2)] = sx
                    last_zadd2[kg % 2] = p.op(
                        "vector", lambda e, zcol=zcol, zp=zp: e.tensor_add(zcol, zcol, zp),
                        waits=[(sp, sx)], inc=vp)
                p.op("vector", lambda e, zcol=zcol, j=j, h=h: e.reciprocal(
                    ZrL[:, 2 * j + h:2 * j + h + 1], zcol))
                s_prs[h] = sp.n
            v_zr2 = vp.n
            if pendD is not None:
                d_tail(pendD)
            t_avx = tn.n
            for h in range(2):
                tspX = tspA if h == 0 else tspB
                psrc = VT[:, 0:T] if h == 0 else A[:, 24576:24576 + T]
                p.op("sync", lambda e, h=h, nk=nk, psrc=psrc: e.dma_start_transpose(
                    XSB[:, h * T: h * T + nk * 128].rearrange("p (t q) -> p t q", t=nk),
                    psrc[:, 0:nk * 128]),
                    waits=[(sp, s_prs[h]), (tn, t_avx)], inc=tspX, inc_k=16)
            pendD = (j, nk, v_zr2)
        d_tail(pendD)
    v_y_done = vp.n

    if debug >= 3:
        for c in range(8):
            p.op("gpsimd", lambda e, c=c: e.dma_start(
                out=dbg["y"][c], in_=B[:, 16384 + c * 2048: 16384 + (c + 1) * 2048]),
                waits=[(vp, v_y_done)], inc=sod, inc_k=16)
        p.wait("vector", sod, sod.n)

    p.mark("D_attn1")
    wgm = CSTB[:, 136:200].rearrange("p (c e) -> p c e", c=8)
    lwg = p.op("gpsimd", lambda e: e.dma_start(
        out=wgm, in_=WgT_moe.rearrange("(c p) e -> p c e", p=128)),
        waits=[(sp, s_q)], inc=gd, inc_k=16)
    eps_r = CST[0:1, 776:784]
    lep = p.op("gpsimd", lambda e: e.dma_start(out=eps_r, in_=eps_row_in),
               waits=[(sp, s_q)], inc=gd, inc_k=16)
    ws_be = WS[0:8, 0:1024]
    lbe = p.op("gpsimd", lambda e: e.dma_start(out=ws_be, in_=be_in),
               waits=[(sp, s_q)], inc=gd, inc_k=16)
    wet2 = [WS[:, 1024:2048].rearrange("p (c o) -> p c o", c=8),
            KQ[:, 0:1024].rearrange("p (c o) -> p c o", c=8)]
    epsbc = SM[:, 2592:2600]
    logi = SM[:, 2600:2608]
    lp1 = SM[:, 2608:2616]
    mask1 = SM[:, 2616:2624]
    mcol1 = SM[:, 2624:2625]
    mcol2 = SM[:, 2625:2626]
    e2c = SM[:, 2626:2627]
    w1c = SM[:, 2627:2628]
    w2c = SM[:, 2628:2629]
    r3E = SM[0:1, 0:512]
    crows = SM.bitcast(BF16)[0:8, 5264:5776]
    crow1 = r1
    x2c = SM[:, 512:1024]
    tb_e = p.op("tensor", lambda e: e.matmul(
        pbank[7][:, 0:8], ones_row, eps_r, start=True, stop=True),
        waits=[(gd, lep), (vp, v_y_done)], inc=tn)
    p.op("vector", lambda e: e.tensor_copy(epsbc, pbank[7][:, 0:8]), waits=[(tn, tb_e)])

    mm_hist = []
    for tgo in range(4):
        x1v = kqv[:, :, tgo * TG:(tgo + 1) * TG]
        v_xn1o = ln_chunk(x1v, vtx, [(vp, vp.n), (sp, sp.n), (tn, tn.n), (so, 0)], ones_x=ones_col_b, r3x=r3E)
        for sub in range(4):
            ccol = SM[:, 2560 + sub * 8: 2568 + sub * 8]
            for c in range(8):
                p.op("tensor", lambda e, c=c, sub=sub: e.matmul(
                    pbank[6][:, 0:8], vtx[:, c, sub * 128:(sub + 1) * 128], wgm[:, c, :],
                    start=(c == 0), stop=(c == 7)),
                    waits=[(gd, lwg), (vp, v_xn1o), (vp, vp.n)] if c == 0 else (), inc=tn)
            t_lg = tn.n
            p.op("vector", lambda e: e.tensor_copy(logi, pbank[6][:, 0:8]), waits=[(tn, t_lg)])
            p.op("vector", lambda e: e.tensor_add(lp1, logi, epsbc))
            p.op("vector", lambda e: e.reduce_max(mcol1, lp1, axis=AX.X))
            p.op("vector", lambda e: e.tensor_scalar(mask1, lp1, mcol1[:, 0:1], None, ALU.is_ge))
            p.op("vector", lambda e: e.scalar_tensor_tensor(
                lp1, mask1, -1.0e9, lp1, ALU.mult, ALU.add))
            p.op("vector", lambda e: e.reduce_max(mcol2, lp1, axis=AX.X))
            p.op("vector", lambda e, ccol=ccol: e.tensor_scalar(ccol, lp1, mcol2[:, 0:1], None, ALU.is_ge))
            p.op("vector", lambda e: e.reduce_max(mcol1, logi, axis=AX.X))
            p.op("vector", lambda e: e.scalar_tensor_tensor(
                lp1, mask1, -1.0e9, logi, ALU.mult, ALU.add))
            p.op("vector", lambda e: e.reduce_max(mcol2, lp1, axis=AX.X))
            p.op("vector", lambda e: e.tensor_tensor(e2c, mcol2, mcol1, ALU.subtract))
            v_pre = vp.n
            p.op("scalar", lambda e: e.activation(e2c, e2c, AF.Exp), waits=[(vp, v_pre)])
            s_e2 = sp.n
            p.op("vector", lambda e: e.tensor_scalar(w1c, e2c, 1.0, None, ALU.add),
                 waits=[(sp, s_e2)])
            p.op("vector", lambda e: e.reciprocal(w1c, w1c))
            p.op("vector", lambda e: e.tensor_tensor(w2c, e2c, w1c, ALU.mult))
            p.op("vector", lambda e: e.tensor_scalar(mask1, mask1, w1c[:, 0:1], None, ALU.mult))
            p.op("vector", lambda e, ccol=ccol: e.scalar_tensor_tensor(
                ccol, ccol, w2c[:, 0:1], mask1, ALU.mult, ALU.add))
            v_coef = vp.n
            tc = p.op("tensor", lambda e, ccol=ccol: e.transpose(
                pbank[7][0:8, 0:128], ccol, ident_f), waits=[(vp, v_coef)], inc=tn)
            p.op("vector", lambda e, sub=sub: e.tensor_copy(
                crows[:, sub * 128:(sub + 1) * 128], pbank[7][0:8, 0:128]),
                waits=[(tn, tc)])
        v_crows = vp.n
        rhse2 = [XSB[:, 0:2048].rearrange("p (c t) -> p c t", c=8),
                 XSB[:, 2048:4096].rearrange("p (c t) -> p c t", c=8)]
        g1v = XSB[:, 4096:8192].rearrange("p (c t) -> p c t", c=8)
        for tp in range(2):
            toff = tp * 256
            for och in range(8):
                p.op("tensor", lambda e, och=och, toff=toff: e.matmul(
                    pbank[och % 4][:, (och // 4) * 256:(och // 4) * 256 + 256],
                    ws_be[:, och * 128:(och + 1) * 128], crows[:, toff:toff + 256],
                    start=(och < 4), stop=False, skip_group_check=True),
                    waits=[(gd, lbe), (vp, v_crows), (sp, sp.n)] if och == 0 else (), inc=tn)
            for e_ in range(E):
                rhse = rhse2[e_ % 2]
                tmpCe = tmpC[:, 0:256] if e_ % 2 == 0 else tmpC[:, 256:512]
                for si in range(2):
                    sub = tp * 2 + si
                    ccol = SM[:, 2560 + sub * 8: 2568 + sub * 8]
                    p.op("tensor", lambda e, e_=e_, si=si, ccol=ccol: e.matmul(
                        pbank[7][0:1, si * 128:(si + 1) * 128], ccol[:, e_:e_ + 1], ident_f,
                        start=(si == 0), stop=(si == 1)),
                        waits=[(vp, vp.n)] if si == 0 else (), inc=tn)
                t_cr = tn.n
                vr = p.op("vector", lambda e: e.tensor_copy(crow1[0:1, 0:256], pbank[7][0:1, 0:256]),
                          waits=[(tn, t_cr)], inc=vp)
                tbx = p.op("tensor", lambda e: e.matmul(
                    pbank[7][:, 256:512], ones_row, crow1[0:1, 0:256], start=True, stop=True),
                    waits=[(vp, vr)], inc=tn)
                vb = p.op("vector", lambda e, tmpCe=tmpCe: e.tensor_copy(tmpCe, pbank[7][:, 256:512]),
                          waits=[(tn, tbx)], inc=vp)
                for c in range(8):
                    w_m = []
                    if c == 0:
                        w_m = [(tn, mm_hist[-9])] if len(mm_hist) >= 9 else [(tn, tn.n)]
                    p.op("vector", lambda e, c=c, toff=toff, rhse=rhse, tmpCe=tmpCe: e.tensor_mul(
                        rhse[:, c, :], vtx[:, c, toff:toff + 256], tmpCe),
                        waits=w_m)
                v_rh = vp.n
                for och in range(8):
                    wet = wet2[och % 2]
                    chn = gdA if och % 2 == 0 else gdB
                    if len(mm_hist) >= 2:
                        w_l = [(tn, mm_hist[-2])]
                    else:
                        w_l = [(tn, tn.n)]
                    lwe = p.op("gpsimd", lambda e, och=och, e_=e_, wet=wet: e.dma_start(
                        out=wet[:, :, 0:128],
                        in_=WeT[e_].rearrange("(c p) o -> p c o", p=128)[:, :, och * 128:(och + 1) * 128]),
                        waits=w_l, inc=chn, inc_k=16)
                    for c in range(8):
                        p.op("tensor", lambda e, c=c, och=och, toff=toff, wet=wet, rhse=rhse: e.matmul(
                            pbank[och % 4][:, (och // 4) * 256:(och // 4) * 256 + 256],
                            wet[:, c, 0:128], rhse[:, c, :],
                            start=False, stop=False, skip_group_check=True),
                            waits=[(chn, lwe), (vp, v_rh)] if c == 0 else (), inc=tn)
                    mm_hist.append(tn.n)
            for och in range(8):
                p.op("tensor", lambda e, och=och: e.matmul(
                    pbank[och % 4][:, (och // 4) * 256:(och // 4) * 256 + 256],
                    ones_row, zrow512[0:1, 0:256], start=False, stop=True,
                    skip_group_check=True), inc=tn)
            t_moe = tn.n
            for och in range(8):
                p.op("scalar", lambda e, och=och, toff=toff: e.activation(
                    g1v[:, och, toff:toff + 256],
                    pbank[och % 4][:, (och // 4) * 256:(och // 4) * 256 + 256],
                    AF.Silu), waits=[(tn, t_moe)] if och == 0 else (), inc=sp)
        s_g1 = sp.n
        if debug >= 4:
            for c in range(8):
                p.op("gpsimd", lambda e, c=c, tgo=tgo: e.dma_start(
                    out=dbg["g1"][c][:, tgo * TG:(tgo + 1) * TG], in_=g1v[:, c, :]),
                    waits=[(sp, s_g1)], inc=sod, inc_k=16)
            p.wait("scalar", sod, sod.n)
            p.wait("vector", sod, sod.n)
        for c in range(8):
            p.op("vector", lambda e, c=c, tgo=tgo: e.tensor_mul(
                tmpA, B[:, 16384 + c * 2048 + tgo * TG: 16384 + c * 2048 + (tgo + 1) * TG],
                g1v[:, c, :]), waits=[(sp, s_g1)] if c == 0 else ())
            p.op("vector", lambda e, c=c, tgo=tgo: e.tensor_add(
                x2c, kqv[:, c, tgo * TG:(tgo + 1) * TG], tmpA))
            v_x2 = vp.n
            p.op("sync", lambda e, c=c, tgo=tgo: e.dma_start(
                out=out.rearrange("(c p) t -> p c t", p=128)[:, c, tgo * TG:(tgo + 1) * TG],
                in_=x2c), waits=[(vp, v_x2)], inc=so, inc_k=16)
            p.wait("vector", so, so.n)

    def finish(p):
        p.wait("sync", so, so.n)
        p.wait("gpsimd", sod, sod.n)

    p.mark("E_moe")
    return nc, p, prog_tensors, finish


def _close(nc, p, prog_tensors, finish):
    finish(p)
    with nc.Block() as block:
        block.tensor(lambda e: [f(e) for f in p.ops["tensor"]])
        block.vector(lambda e: [f(e) for f in p.ops["vector"]])
        block.scalar(lambda e: [f(e) for f in p.ops["scalar"]])
        block.sync(lambda e: [f(e) for f in p.ops["sync"]])
        block.gpsimd(lambda e: [f(e) for f in p.ops["gpsimd"]])
    for cm in reversed(prog_tensors):
        cm.__exit__(None, None, None)


def run_prog(nc, p, prog_tensors, finish, in_map, cores=1):
    from concourse.bass_utils import run_bass_kernel_spmd
    _close(nc, p, prog_tensors, finish)
    maps = [in_map] * cores if isinstance(in_map, dict) else in_map
    return run_bass_kernel_spmd(nc, maps, core_ids=list(range(cores)))



_CACHE = {}


def kernel(L0_attn_x, L0_attn_w_in, L0_attn_w_out, L0_ffn_w_gate, L0_ffn_w_up,
           L0_ffn_w_down, L1_attn_W_dq, L1_attn_W_uq, L1_attn_W_dkv,
           L1_attn_W_uk, L1_attn_W_uv, L1_attn_W_o, L1_ffn_Wg, L1_ffn_We,
           L1_ffn_be):
    from concourse.bass_utils import run_bass_kernel_spmd

    x = np.asarray(L0_attn_x)
    B_, T_, D_ = x.shape
    w_in = np.ascontiguousarray(L0_attn_w_in)
    tri128 = np.tril(np.ones((128, 128), np.float32))
    NEG = np.float32(NEG_BIG)
    shared = {
        "w_qT": np.ascontiguousarray(w_in[0:1024].T),
        "w_kT": np.ascontiguousarray(w_in[1024:2048].T),
        "w_vT": np.ascontiguousarray(w_in[2048:3072].T),
        "w_outT": np.ascontiguousarray(np.asarray(L0_attn_w_out).T),
        "wgT": np.ascontiguousarray(np.asarray(L0_ffn_w_gate).T),
        "wuT": np.ascontiguousarray(np.asarray(L0_ffn_w_up).T),
        "wdT": np.ascontiguousarray(np.asarray(L0_ffn_w_down).T),
        "tri": tri128,
        "ident": np.eye(128, dtype=np.float32),
        "Wdq": np.ascontiguousarray(L1_attn_W_dq),
        "WdqT": np.ascontiguousarray(np.asarray(L1_attn_W_dq).T),
        "WuqT": np.ascontiguousarray(np.asarray(L1_attn_W_uq).T),
        "Wuq": np.ascontiguousarray(L1_attn_W_uq),
        "Wuk": np.ascontiguousarray(L1_attn_W_uk),
        "Wuv": np.ascontiguousarray(L1_attn_W_uv),
        "WoT": np.ascontiguousarray(np.asarray(L1_attn_W_o).T),
        "WdkvT": np.ascontiguousarray(np.asarray(L1_attn_W_dkv).T),
        "WgT_moe": np.ascontiguousarray(np.asarray(L1_ffn_Wg).T),
        "WeT": np.ascontiguousarray(np.asarray(L1_ffn_We).transpose(0, 2, 1)),
        "be": np.ascontiguousarray(L1_ffn_be),
        "sel2_in": np.kron(np.eye(2, dtype=np.float32), np.ones((1, 64), np.float32)),
        "eps_row_in": (np.arange(E, dtype=np.float32) * -EPS_TIE)[None, :],
    }
    mk = {
        0: {"mkA": np.where(tri128 > 0, 0.0, NEG).astype(np.float32),
            "mkB": np.full((128, 128), NEG, np.float32)},
        1: {"mkA": np.zeros((128, 128), np.float32),
            "mkB": np.where(tri128 > 0, 0.0, NEG).astype(np.float32)},
    }
    in_maps = []
    for core in range(8):
        b, u = core // 2, core % 2
        m = dict(shared)
        m["xT"] = np.ascontiguousarray(x[b].T)
        m["u_in"] = np.full((128, 1), float(u), np.float32)
        m.update(mk[u])
        in_maps.append(m)

    if "nc" not in _CACHE:
        nc, p, pt, fin = build(debug=0)
        _close(nc, p, pt, fin)
        _CACHE["nc"] = nc
    nc = _CACHE["nc"]
    res = run_bass_kernel_spmd(nc, in_maps, core_ids=list(range(8)))

    outp = np.empty((B_, T_, D_), np.float32)
    own_idx = {u: (np.arange(16)[:, None] * 256 + u * 128 + np.arange(128)[None, :]).reshape(-1)
               for u in range(2)}
    for core in range(8):
        b, u = core // 2, core % 2
        sh = res.results[core]["out"]
        outp[b, own_idx[u], :] = np.ascontiguousarray(sh.T)
    return outp


# version 09 (final)
# speedup vs baseline: 1.0320x; 1.0320x over previous
import math
import numpy as np

import concourse.bass as bass
import concourse.mybir as mybir

F32 = mybir.dt.float32
BF16 = mybir.dt.bfloat16
I32 = mybir.dt.int32
AF = mybir.ActivationFunctionType
ALU = mybir.AluOpType
AX = mybir.AxisListType

T = 4096
D = 1024
NB = T // 128
H = 16
HD = 64
HIDDEN = 3072
LAT = 256
E = 8
TOWN = T // 2
NBO = TOWN // 128
QSCALE = 0.125
DEQ = 1.0 / 127.0
M_L1 = 2.0
NEG_BIG = -1.0e6
EPS_LN = 1e-5
EPS_TIE = 1e-6


class Ch:
    def __init__(self, nc, name):
        self._cm = nc.semaphore(name)
        self.sem = self._cm.__enter__()
        self.n = 0


class P:
    def __init__(self, nc):
        self.nc = nc
        self.ops = {"tensor": [], "vector": [], "scalar": [], "sync": [], "gpsimd": []}
        self.eng_ch = {}
        self.dma_chs = set()

    def emit(self, eng, fn):
        self.ops[eng].append(fn)

    def op(self, eng, body, waits=(), inc=None, inc_k=1, self_wait=True):
        waits = [(ch, ch.n if ch in self.dma_chs else v) for (ch, v) in waits]
        ch0 = self.eng_ch.get(eng)
        if ch0 is not None:
            if self_wait and ch0.n > 0:
                waits.append((ch0, ch0.n))
            if inc is None:
                inc = ch0
            assert inc is ch0, f"{eng} ops must inc their engine channel"
        def fn(e, waits=waits, body=body, inc=inc, inc_k=inc_k):
            for ch, v in waits:
                e.wait_ge(ch.sem, v)
            inst = body(e)
            if inc is not None:
                assert inst is not None, "op body must return its instruction"
                inst.then_inc(inc.sem, inc_k)
        if inc is not None:
            inc.n += inc_k
        self.emit(eng, fn)
        return inc.n if inc is not None else None

    def wait(self, eng, ch, v=None):
        vv = ch.n if v is None else v
        self.emit(eng, lambda e: e.wait_ge(ch.sem, vv))

    def mark(self, label):
        if not hasattr(self, "marks"):
            self.marks = []
        self.marks.append((label, {k: len(v) for k, v in self.ops.items()}))


def build(debug=0):
    nc = bass.Bass("TRN2", target_bir_lowering=False)
    TG = 512
    NTG = T // TG

    xT = nc.dram_tensor("xT", [D, T], F32, kind="ExternalInput").ap()
    w_qT = nc.dram_tensor("w_qT", [D, D], F32, kind="ExternalInput").ap()
    w_kT = nc.dram_tensor("w_kT", [D, D], F32, kind="ExternalInput").ap()
    w_vT = nc.dram_tensor("w_vT", [D, D], F32, kind="ExternalInput").ap()
    w_outT = nc.dram_tensor("w_outT", [D, D], F32, kind="ExternalInput").ap()
    wgT = nc.dram_tensor("wgT", [D, HIDDEN], I32, kind="ExternalInput").ap()
    wuT = nc.dram_tensor("wuT", [D, HIDDEN], I32, kind="ExternalInput").ap()
    wdT = nc.dram_tensor("wdT", [HIDDEN, D], I32, kind="ExternalInput").ap()
    tri = nc.dram_tensor("tri", [128, 128], F32, kind="ExternalInput").ap()
    ident_in = nc.dram_tensor("ident", [128, 128], F32, kind="ExternalInput").ap()
    Wdq = nc.dram_tensor("Wdq", [256, D], F32, kind="ExternalInput").ap()
    WdqT = nc.dram_tensor("WdqT", [D, 256], F32, kind="ExternalInput").ap()
    WuqT = nc.dram_tensor("WuqT", [256, D], F32, kind="ExternalInput").ap()
    Wuq = nc.dram_tensor("Wuq", [D, 256], F32, kind="ExternalInput").ap()
    Wuk = nc.dram_tensor("Wuk", [D, 256], F32, kind="ExternalInput").ap()
    Wuv = nc.dram_tensor("Wuv", [D, 256], F32, kind="ExternalInput").ap()
    WoT = nc.dram_tensor("WoT", [D, D], F32, kind="ExternalInput").ap()
    WdkvT = nc.dram_tensor("WdkvT", [D, 256], F32, kind="ExternalInput").ap()
    WgT_moe = nc.dram_tensor("WgT_moe", [D, E], F32, kind="ExternalInput").ap()
    WeT = nc.dram_tensor("WeT", [E, D, D], F32, kind="ExternalInput").ap()
    be_in = nc.dram_tensor("be", [E, D], F32, kind="ExternalInput").ap()
    mkA = nc.dram_tensor("mkA", [128, 128], F32, kind="ExternalInput").ap()
    mkB = nc.dram_tensor("mkB", [128, 128], F32, kind="ExternalInput").ap()
    u_in = nc.dram_tensor("u_in", [128, 1], F32, kind="ExternalInput").ap()
    sel2_in = nc.dram_tensor("sel2_in", [2, 128], F32, kind="ExternalInput").ap()
    eps_row_in = nc.dram_tensor("eps_row_in", [1, E], F32, kind="ExternalInput").ap()

    out = nc.dram_tensor("out", [D, TOWN], F32, kind="ExternalOutput").ap()
    dbg = {}
    if debug >= 1:
        dbg["xn"] = nc.dram_tensor("dbg_xn", [8, 128, T], F32, kind="ExternalOutput").ap()
    if debug >= 2:
        dbg["ctx"] = nc.dram_tensor("dbg_ctx", [8, 128, T], F32, kind="ExternalOutput").ap()
    if debug >= 3:
        dbg["x1o"] = nc.dram_tensor("dbg_x1o", [8, 128, TOWN], F32, kind="ExternalOutput").ap()
        dbg["ckv"] = nc.dram_tensor("dbg_ckv", [32, 128, LAT], F32, kind="ExternalOutput").ap()
        dbg["q"] = nc.dram_tensor("dbg_q", [8, 128, TOWN], F32, kind="ExternalOutput").ap()
        dbg["y"] = nc.dram_tensor("dbg_y", [8, 128, TOWN], F32, kind="ExternalOutput").ap()
    if debug >= 4:
        dbg["g1"] = nc.dram_tensor("dbg_g1", [8, 128, TOWN], F32, kind="ExternalOutput").ap()

    prog_tensors = []

    def sb(name, shape, dt):
        cm = nc.sbuf_tensor(name, shape, dt)
        t = cm.__enter__()
        prog_tensors.append(cm)
        return t

    def ps(name, shape, dt=F32):
        cm = nc.psum_tensor(name, shape, dt)
        t = cm.__enter__()
        prog_tensors.append(cm)
        return t

    A = sb("A", [128, 8 * T], BF16)
    B = sb("B", [128, 8 * T], BF16)
    KQ = sb("KQ", [128, 2 * T], BF16)
    VT = sb("VT", [128, T], BF16)
    PT = sb("PT", [128, 2 * 33 * 128], BF16)
    WS = sb("WS", [128, 2 * 1024], BF16)
    XS = sb("XS", [128, 8 * TG], F32)
    SM = sb("SM", [128, 3072], F32)
    CST = sb("CST", [128, 800], F32)
    CSTB = sb("CSTB", [128, 256], BF16)
    PB = XS.bitcast(BF16)[:, 0:T]

    pbank = [ps(f"pb{i}", [128, 512]) for i in range(8)]

    ident_f = CST[:, 0:128]
    tri_f = CST[:, 128:256]
    mkA_f = CST[:, 256:384]
    mkB_f = CST[:, 384:512]
    ones_col = CST[:, 512:513]
    u_col = CST[:, 513:514]
    zero_col = CST[:, 514:515]
    negml1_col = CST[:, 515:516]
    ones_row = CST[0:1, 520:648]
    sel2 = CST[0:2, 648:776]
    ident_b = CSTB[:, 0:128]
    ones_col_b = CSTB[:, 128:129]

    p = P(nc)
    ld = Ch(nc, "ld")
    tsp = Ch(nc, "tsp")
    tspA = Ch(nc, "tspA")
    tspB = Ch(nc, "tspB")
    tn = Ch(nc, "tn")
    vp = Ch(nc, "vp")
    sp = Ch(nc, "sp")
    gd = Ch(nc, "gd")
    gdA = Ch(nc, "gdA")
    gdB = Ch(nc, "gdB")
    so = Ch(nc, "so")
    sod = Ch(nc, "sod")
    p.eng_ch = {"vector": vp, "scalar": sp}
    p.dma_chs = {ld, tsp, tspA, tspB, gd, gdA, gdB, so, sod}

    p.op("gpsimd", lambda e: e.dma_start(out=ident_f, in_=ident_in), inc=gd, inc_k=16)
    p.op("gpsimd", lambda e: e.dma_start(out=ident_b, in_=ident_in), inc=gd, inc_k=16)
    p.op("gpsimd", lambda e: e.dma_start(out=tri_f, in_=tri), inc=gd, inc_k=16)
    p.op("gpsimd", lambda e: e.dma_start(out=mkA_f, in_=mkA), inc=gd, inc_k=16)
    p.op("gpsimd", lambda e: e.dma_start(out=mkB_f, in_=mkB), inc=gd, inc_k=16)
    p.op("gpsimd", lambda e: e.dma_start(out=u_col, in_=u_in), inc=gd, inc_k=16)
    p.op("gpsimd", lambda e: e.dma_start(out=sel2, in_=sel2_in), inc=gd, inc_k=16)
    consts_gd = gd.n
    p.op("vector", lambda e: e.memset(ones_col, 1.0), inc=vp)
    p.op("vector", lambda e: e.memset(ones_col_b, 1.0), inc=vp)
    p.op("vector", lambda e: e.memset(zero_col, 0.0), inc=vp)
    p.op("vector", lambda e: e.memset(negml1_col, -M_L1), inc=vp)
    p.op("vector", lambda e: e.memset(ones_row, 1.0), inc=vp)
    consts_v = vp.n
    p.wait("tensor", gd, consts_gd)
    p.wait("scalar", gd, consts_gd)
    p.wait("vector", gd, consts_gd)

    def xn_c(c):
        return A[:, c * T:(c + 1) * T]

    sq_buf = [SM[:, 0:512], SM[:, 512:1024]]
    tmpA = SM[:, 1024:1536]
    r1 = SM[0:1, 1536:2048]
    r2 = SM[0:1, 2048:2560]
    r3 = SM[0:1, 2560:3072]

    prev_consumed = 0
    for g in range(NTG):
        xsv = XS[:].rearrange("p (c t) -> p c t", c=8)
        src = xT.rearrange("(c p) t -> p c t", p=128)[:, :, g * TG:(g + 1) * TG]
        ldv = p.op("sync", lambda e, s=src, d=xsv: e.dma_start(out=d, in_=s),
                   waits=[(vp, prev_consumed)] if g > 0 else (), inc=ld, inc_k=16)
        for c in range(8):
            p.op("tensor", lambda e, c=c, xsv=xsv: e.matmul(
                pbank[6][0:1, 0:TG], ones_col, xsv[:, c, :], start=(c == 0), stop=(c == 7)),
                waits=[(ld, ldv), (vp, consts_v)] if c == 0 else (), inc=tn)
        t_sum = tn.n
        sq_mm = {}
        for c in range(8):
            sq = sq_buf[c % 2]
            w = [(ld, ldv)] if c == 0 else []
            if c >= 2:
                w.append((tn, sq_mm[c - 2]))
            sv = p.op("scalar", lambda e, c=c, xsv=xsv, sq=sq: e.square(sq, xsv[:, c, :]),
                      waits=w, inc=sp)
            sq_mm[c] = p.op("tensor", lambda e, c=c, sq=sq: e.matmul(
                pbank[7][0:1, 0:TG], ones_col, sq, start=(c == 0), stop=(c == 7)),
                waits=[(sp, sv), (tn, t_sum)] if c == 0 else [(sp, sv)], inc=tn)
        t_sq = tn.n

        p.op("vector", lambda e: e.tensor_scalar(r1, pbank[6][0:1, 0:TG], 1.0 / D, None, ALU.mult),
             waits=[(tn, t_sq)])
        p.op("vector", lambda e: e.tensor_scalar(r2, pbank[7][0:1, 0:TG], 1.0 / D, None, ALU.mult))
        p.op("vector", lambda e: e.tensor_tensor(r3, r1, r1, ALU.mult))
        p.op("vector", lambda e: e.tensor_tensor(r2, r2, r3, ALU.subtract))
        p.op("vector", lambda e: e.tensor_scalar(r2, r2, EPS_LN, None, ALU.add))
        v_stats = vp.n
        p.op("scalar", lambda e: e.sqrt(r2, r2), waits=[(vp, v_stats)], inc=sp)
        s_sqrt = sp.n

        p.op("vector", lambda e: e.reciprocal(r2, r2), waits=[(sp, s_sqrt)])
        p.op("vector", lambda e: e.tensor_tensor(r1, r1, r2, ALU.mult))
        p.op("vector", lambda e: e.tensor_scalar(r1, r1, -1.0, None, ALU.mult))
        v_ab = vp.n
        p.op("tensor", lambda e: e.matmul(pbank[4][:, 0:TG], ones_row, r2, start=True, stop=True),
             waits=[(vp, v_ab)], inc=tn)
        p.op("tensor", lambda e: e.matmul(pbank[5][:, 0:TG], ones_row, r1, start=True, stop=True),
             inc=tn)
        t_bc = tn.n
        for c in range(8):
            p.op("vector", lambda e, c=c, xsv=xsv: e.tensor_tensor(
                tmpA, xsv[:, c, :], pbank[4][:, 0:TG], ALU.mult),
                waits=[(tn, t_bc)] if c == 0 else ())
            p.op("vector", lambda e, c=c, g=g: e.tensor_tensor(
                xn_c(c)[:, g * TG:(g + 1) * TG], tmpA, pbank[5][:, 0:TG], ALU.add))
        prev_consumed = vp.n
        if g < NTG - 1:
            p.wait("tensor", vp, prev_consumed)
    v_xn_done = vp.n

    if debug >= 1:
        for c in range(8):
            p.op("gpsimd", lambda e, c=c: e.dma_start(out=dbg["xn"][c], in_=xn_c(c)),
                 waits=[(vp, v_xn_done)], inc=sod, inc_k=16)

    p.mark("A_ln")
    XSB = XS.bitcast(BF16)
    SMB = SM.bitcast(BF16)
    KA = [KQ[:, 0:T], KQ[:, T:2 * T]]
    QA = [VT[:, 0:T], XSB[:, T:2 * T]]
    PB2 = [XSB[:, 0:512], XSB[:, 512:1024]]
    VTA = PT[:, 0:T]
    PBK = [[PT[:, T:T + 512], PT[:, T + 512:T + 1024]],
           [PT[:, T + 1024:T + 1536], PT[:, T + 1536:T + 2048]]]
    VSufR = [PT[0:1, 6144:8192], PT[64:65, 6144:8192]]
    EMR = [SMB[0:1, 2048:6144], SMB[64:65, 2048:6144]]
    Mt = SM[:, 0:64]
    mtmp = SM[:, 64:65]
    mtcols = SM[:, 1004:1012]
    negMtb = SMB[:, 132:196]
    negMtf = SM[:, 100:164]
    EMtf = SM[:, 164:228]
    stackf = SM[:, 100:228]
    SMt = SM[:, 100:228]
    triT_f = SM[:, 228:356]
    VSb2 = [SM[0:64, 356:389], SM[0:64, 389:422]]
    VSuf2 = [SM[0:64, 422:455], SM[0:64, 455:488]]
    VSufT = [SM[0:32, 488:552], SM[0:32, 552:616]]
    zrow512b = SMB[0:1, 1232:1744]
    ones64f = SM[64:65, 872:1000]
    onesrb = SMB[0:1, 1744:1872]
    zbcs = XS[:, 512:1024]
    Zfin = [XS[0:1, 1024:1536], XS[64:65, 1024:1536]]
    zbc = SM[:, 640:768]
    zr2 = SM[0:2, 1792:1920]
    wbuf = [WS[:, 0:1024].rearrange("p (c e) -> p c e", c=8),
            WS[:, 1024:2048].rearrange("p (c e) -> p c e", c=8)]

    def ctx_c(hg):
        return B[:, hg * T:(hg + 1) * T]

    def load_whg(wsrc, hg, slot, waits=()):
        src = wsrc.rearrange("(c p) e -> p c e", p=128)[:, :, hg * 128:(hg + 1) * 128]
        chn = gdA if slot == 0 else gdB
        p.op("gpsimd", lambda e, s=src, d=wbuf[slot]: e.dma_start(out=d, in_=s),
             waits=waits, inc=chn, inc_k=16)
        return chn

    if debug >= 5:
        p.op("vector", lambda e: e.memset(KQ[:, 0:2 * T], 0.0),
             waits=[(vp, v_xn_done), (sp, sp.n)], inc=vp)
        p.op("vector", lambda e: e.memset(PT[:, 0:8448], 0.0), inc=vp)
        p.op("vector", lambda e: e.memset(SM[:, 0:3072], 0.0), inc=vp)
        p.op("vector", lambda e: e.memset(VT[:, 0:T], 0.0), inc=vp)
        p.op("vector", lambda e: e.memset(XS[:, 0:4096], 0.0), inc=vp)
    p.op("vector", lambda e: e.memset(KQ[64:65, 0:2 * T], 1.0),
         waits=[(vp, v_xn_done), (sp, sp.n)], inc=vp)
    p.op("vector", lambda e: e.memset(zrow512b, 0.0), inc=vp)
    p.op("vector", lambda e: e.memset(ones64f, 1.0), inc=vp)
    p.op("vector", lambda e: e.memset(onesrb, 1.0), inc=vp)
    v_binit0 = vp.n
    t_tri = p.op("tensor", lambda e: e.transpose(pbank[5][:, 0:128], tri_f, ident_f),
                 waits=[(vp, v_binit0)], inc=tn)
    p.op("vector", lambda e: e.tensor_copy(triT_f, pbank[5][:, 0:128]),
         waits=[(tn, t_tri)], inc=vp)
    v_binit = vp.n

    stgKQ = [SMB[64:128, 712:1224], SMB[64:128, 1232:1744]]

    def proj_split(wslot, wld, dst0, dst1, first_waits):
        ev_by_tg = []
        for tg in range(NTG):
            bank = pbank[4 + 2 * (tg % 2)]
            for c in range(8):
                wts = []
                if c == 0:
                    if tg == 0:
                        wts = [(wld, 0)] + list(first_waits)
                    elif tg >= 2:
                        wts = [(sp, ev_by_tg[tg - 2])]
                p.op("tensor", lambda e, c=c, tg=tg, bank=bank, wslot=wslot: e.matmul(
                    bank[:, 0:TG], wbuf[wslot][:, c, :],
                    xn_c(c)[:, tg * TG:(tg + 1) * TG], start=(c == 0), stop=(c == 7)),
                    waits=wts, inc=tn)
            t_mm = tn.n
            p.op("scalar", lambda e, tg=tg, bank=bank, dst0=dst0: e.copy(
                dst0(tg), bank[0:64, 0:TG]), waits=[(tn, t_mm)], inc=sp)
            ev1 = p.op("scalar", lambda e, tg=tg, bank=bank: e.copy(
                stgKQ[tg % 2], bank[64:128, 0:TG]), waits=[(gd, 0)], inc=sp)
            p.op("gpsimd", lambda e, tg=tg, dst1=dst1: e.dma_start(
                out=dst1(tg), in_=stgKQ[tg % 2]), waits=[(sp, ev1)], inc=gd, inc_k=16)
            ev_by_tg.append(sp.n)
        return sp.n

    prev_q_done = 0
    prev_v_done = 0
    prev_hg_exp = 0
    prev_hg_tn = 0
    b6_free = b7_free = 0
    av_free = {0: 0, 1: 0}
    last_exp = {}
    last_av = {}
    for hg in range(8):
        ldk = load_whg(w_kT, hg, 0, waits=[(sp, prev_q_done)])
        k_done = proj_split(0, ldk,
                            lambda tg: KA[0][0:64, tg * TG:(tg + 1) * TG],
                            lambda tg: KA[1][0:64, tg * TG:(tg + 1) * TG],
                            [(vp, v_xn_done), (vp, av_free[1]), (vp, v_binit)])
        ldv = load_whg(w_vT, hg, 1, waits=[(sp, prev_v_done)])
        ev_by_tg = []
        tr_by_tg = []
        for tg in range(NTG):
            bank = pbank[4 + (tg % 2)]
            for c in range(8):
                wts = []
                if c == 0:
                    if tg == 0:
                        wts = [(ldv, 0), (sp, k_done)]
                    elif tg >= 2:
                        wts = [(sp, ev_by_tg[tg - 2])]
                p.op("tensor", lambda e, c=c, tg=tg, bank=bank: e.matmul(
                    bank[:, 0:TG], wbuf[1][:, c, :],
                    xn_c(c)[:, tg * TG:(tg + 1) * TG], start=(c == 0), stop=(c == 7)),
                    waits=wts, inc=tn)
            t_mm = tn.n
            evw = [(tn, t_mm)]
            if tg >= 2:
                evw.append((tsp, tr_by_tg[tg - 2]))
            ev = p.op("scalar", lambda e, tg=tg, bank=bank: e.copy(
                PB2[tg % 2], bank[:, 0:TG]), waits=evw, inc=sp)
            ev_by_tg.append(ev)
            w_tr = [(sp, ev)]
            if tg == 0:
                w_tr.append((tn, prev_hg_tn))
            p.op("sync", lambda e, tg=tg, pb=PB2[tg % 2]: e.dma_start_transpose(
                VTA.rearrange("p (t x) -> p t x", x=128)[:, 4 * tg:4 * tg + 4, 0:64],
                pb[0:64, :]), waits=w_tr, inc=tsp, inc_k=16)
            p.op("sync", lambda e, tg=tg, pb=PB2[tg % 2]: e.dma_start_transpose(
                VTA.rearrange("p (t x) -> p t x", x=128)[:, 4 * tg:4 * tg + 4, 64:128],
                pb[64:128, :]), inc=tsp, inc_k=16)
            tr_by_tg.append(tsp.n)
        v_done = sp.n
        t_vtr = tsp.n
        ldq = load_whg(w_qT, hg, 0, waits=[(sp, k_done)])
        q_done = proj_split(0, ldq,
                            lambda tg: QA[0][0:64, tg * TG:(tg + 1) * TG],
                            lambda tg: QA[1][0:64, tg * TG:(tg + 1) * TG],
                            [(sp, v_done)])

        for kb in range(NB):
            for h in range(2):
                wts = []
                if kb == 0 and h == 0:
                    wts = [(tsp, t_vtr), (sp, prev_hg_exp)]
                col = kb if h == 0 else 40 + kb
                p.op("tensor", lambda e, kb=kb, h=h, col=col: e.matmul(
                    pbank[0][0:64, col:col + 1],
                    VTA[:, kb * 128 + 64 * h:kb * 128 + 64 * h + 64], ones_col_b,
                    start=True, stop=True), waits=wts, inc=tn)
        t_vs = tn.n
        p.op("vector", lambda e: e.tensor_copy(VSb2[0][:, 0:32], pbank[0][0:64, 0:32]),
             waits=[(tn, t_vs)], inc=vp)
        p.op("vector", lambda e: e.tensor_copy(VSb2[1][:, 0:32], pbank[0][0:64, 40:72]), inc=vp)
        for h in range(2):
            p.op("vector", lambda e, h=h: e.memset(VSuf2[h][:, 0:1], 0.0), inc=vp)
            p.op("vector", lambda e, h=h: e.tensor_copy(
                VSuf2[h][:, 31:32], VSb2[h][:, 31:32]), inc=vp)
            for i in range(30, 0, -1):
                p.op("vector", lambda e, h=h, i=i: e.tensor_add(
                    VSuf2[h][:, i:i + 1], VSb2[h][:, i:i + 1], VSuf2[h][:, i + 1:i + 2]),
                    inc=vp)
        v_vsuf = vp.n
        for h in range(2):
            tvs = p.op("tensor", lambda e, h=h: e.transpose(
                pbank[1][0:32, h * 64:h * 64 + 64], VSuf2[h][:, 0:32], ident_f[0:64, 0:64]),
                waits=[(vp, v_vsuf)] if h == 0 else (), inc=tn)
            p.op("vector", lambda e, h=h: e.tensor_copy(
                VSufT[h], pbank[1][0:32, h * 64:h * 64 + 64]), waits=[(tn, tvs)], inc=vp)
        v_vsr = vp.n
        for h in range(2):
            p.op("gpsimd", lambda e, h=h: e.dma_start(
                out=VSufR[h].rearrange("p (t c) -> p t c", t=32), in_=VSufT[h]),
                waits=[(vp, v_vsr), (tn, prev_hg_tn)], inc=gd, inc_k=16)
        g_vsr = gd.n

        last_p1 = {}
        for qb in range(NB):
            nkg = (qb + 1 + 3) // 4
            for h in range(2):
                qsl = QA[h][0:64, qb * 128:(qb + 1) * 128]
                mcol = Mt[:, qb + 32 * h:qb + 32 * h + 1]
                for kg in range(nkg):
                    c0 = kg * 512
                    c1 = min((qb + 1) * 128, c0 + 512)
                    W = c1 - c0
                    bank = pbank[2 * h + kg % 2]
                    wts = []
                    if qb == 0 and kg == 0 and h == 0:
                        wts += [(sp, q_done), (vp, v_vsr), (gd, 0)]
                    key = (h, kg % 2)
                    if key in last_p1:
                        wts.append((vp, last_p1[key]))
                    mm = p.op("tensor", lambda e, qsl=qsl, h=h, c0=c0, c1=c1, bank=bank, W=W: e.matmul(
                        bank[:, 0:W], qsl, KA[h][0:64, c0:c1],
                        start=True, stop=True), waits=wts, inc=tn)
                    is_diag = (c1 == (qb + 1) * 128)
                    if is_diag:
                        p.op("vector", lambda e, bank=bank, W=W: e.tensor_mul(
                            bank[:, W - 128:W], bank[:, W - 128:W], tri_f),
                            waits=[(tn, mm)], inc=vp)
                    vv = p.op("vector", lambda e, bank=bank, W=W, kg=kg: e.reduce_max(
                        mtcols[:, kg:kg + 1], bank[:, 0:W], axis=AX.X),
                        waits=[(tn, mm)], inc=vp)
                    last_p1[(h, kg % 2)] = vv
                p.op("vector", lambda e, mcol=mcol, nkg=nkg: e.reduce_max(
                    mcol, mtcols[:, 0:nkg], axis=AX.X), inc=vp)
                p.op("vector", lambda e, mcol=mcol: e.tensor_scalar_max(mcol, mcol, 0.0), inc=vp)
        v_m = vp.n

        p.op("vector", lambda e: e.tensor_scalar(negMtb, Mt, -1.0, None, ALU.mult), inc=vp)
        p.op("vector", lambda e: e.tensor_copy(negMtf, negMtb), inc=vp)
        v_nm = vp.n
        s_em = p.op("scalar", lambda e: e.activation(EMtf, negMtb, AF.Exp, scale=QSCALE),
                    waits=[(vp, v_nm)], inc=sp)
        t_mt = p.op("tensor", lambda e: e.transpose(pbank[5][:, 0:128], stackf, ident_f),
                    waits=[(vp, v_nm), (sp, s_em), (vp, av_free[1])], inc=tn)
        ev_smt = p.op("vector", lambda e: e.tensor_copy(SMt, pbank[5][:, 0:128]),
                      waits=[(tn, t_mt)], inc=vp)
        av_free[1] = ev_smt
        for (ar0, ar1, adst) in [(0, 32, QA[0][64:65, 0:T]), (32, 64, QA[1][64:65, 0:T]),
                                 (64, 96, EMR[0]), (96, 128, EMR[1])]:
            p.op("gpsimd", lambda e, ar0=ar0, ar1=ar1, adst=adst: e.dma_start(
                out=adst.rearrange("p (t c) -> p t c", t=32), in_=SMt[ar0:ar1, :]),
                waits=[(vp, ev_smt), (tn, prev_hg_tn)], inc=gd, inc_k=16)
        g_aug = gd.n

        pendG = None

        def g_tail(gt, avb):
            nonlocal b6_free, b7_free
            for qi in range(4):
                qb = 4 * gt + qi
                if qb > 30:
                    continue
                w0 = [(gd, g_vsr)] if (gt == 0 and qi == 0) else []
                p.op("tensor", lambda e, qb=qb, qi=qi, avb=avb: e.matmul(
                    avb[0:64, qi * 128:(qi + 1) * 128],
                    VSufR[0][0:1, (qb + 1) * 64:(qb + 2) * 64],
                    EMR[0][0:1, qb * 128:(qb + 1) * 128],
                    start=False, stop=False, skip_group_check=True), waits=w0, inc=tn)
                p.op("tensor", lambda e, qb=qb, qi=qi, avb=avb: e.matmul(
                    avb[64:128, qi * 128:(qi + 1) * 128],
                    VSufR[1][0:1, (qb + 1) * 64:(qb + 2) * 64],
                    EMR[1][0:1, qb * 128:(qb + 1) * 128],
                    start=False, stop=False, skip_group_check=True,
                    tile_position=(64, 64)), inc=tn)
            p.op("tensor", lambda e, avb=avb: e.matmul(
                avb[:, 0:512], onesrb[0:1, 0:128], zrow512b,
                start=False, stop=True, skip_group_check=True), inc=tn)
            p.op("tensor", lambda e: e.matmul(
                pbank[6][:, 0:512], onesrb[0:1, 0:128], zrow512b,
                start=False, stop=True, skip_group_check=True), inc=tn)
            t_close = tn.n
            first_z = True
            for h in range(2):
                for qi in range(4):
                    qb = 4 * gt + qi
                    cnt = float(T - (qb + 1) * 128)
                    w = [(tn, t_close)] if first_z else []
                    first_z = False
                    p.op("vector", lambda e, h=h, qi=qi, qb=qb, cnt=cnt: e.scalar_tensor_tensor(
                        Zfin[h][0:1, qi * 128:(qi + 1) * 128],
                        EMR[h][0:1, qb * 128:(qb + 1) * 128], cnt,
                        pbank[6][64 * h:64 * h + 1, qi * 128:(qi + 1) * 128],
                        ALU.mult, ALU.add), waits=w, inc=vp)
                p.op("vector", lambda e, h=h: e.reciprocal(Zfin[h], Zfin[h]), inc=vp)
            v_zr = vp.n
            b6_free = v_zr
            p.op("tensor", lambda e: e.matmul(
                pbank[7][:, 0:512], onesrb[0:1, 0:128], zrow512b, start=True, stop=False,
                skip_group_check=True), waits=[(vp, b7_free)], inc=tn)
            p.op("tensor", lambda e: e.matmul(
                pbank[7][0:64, 0:512], ones_row[0:1, 0:64], Zfin[0],
                start=False, stop=False, skip_group_check=True), waits=[(vp, v_zr)], inc=tn)
            p.op("tensor", lambda e: e.matmul(
                pbank[7][64:128, 0:512], ones64f[0:1, 0:64], Zfin[1],
                start=False, stop=False, skip_group_check=True, tile_position=(64, 64)), inc=tn)
            p.op("tensor", lambda e: e.matmul(
                pbank[7][:, 0:512], onesrb[0:1, 0:128], zrow512b,
                start=False, stop=True, skip_group_check=True), inc=tn)
            t_zbc = tn.n
            ev_zbc = p.op("vector", lambda e: e.tensor_copy(zbcs, pbank[7][:, 0:512]),
                          waits=[(tn, t_zbc)], inc=vp)
            b7_free = ev_zbc
            p.op("vector", lambda e, gt=gt, hg=hg, avb=avb: e.tensor_mul(
                ctx_c(hg)[0:64, gt * 512:(gt + 1) * 512], avb[0:64, 0:512],
                zbcs[0:64, :]), waits=[(tn, t_close)], inc=vp)
            p.op("vector", lambda e, gt=gt, hg=hg, avb=avb: e.tensor_mul(
                ctx_c(hg)[64:128, gt * 512:(gt + 1) * 512], avb[64:128, 0:512],
                zbcs[64:128, :]), inc=vp)
            av_free[gt % 2] = vp.n

        for g in range(8):
            kmax = 4 * g + 3
            avb = pbank[4 + g % 2]

            def emit_avz(kb2, c02, pks, avb=avb):
                for h2 in range(2):
                    pk2 = pks[h2]
                    avw = [(sp, pk2[1])]
                    if h2 == 0:
                        p.op("tensor", lambda e, kb2=kb2, c02=c02, pk2=pk2, avb=avb: e.matmul(
                            avb[0:64, c02:512], VTA[:, kb2 * 128:kb2 * 128 + 64],
                            pk2[0][:, c02:512], start=False, stop=False, skip_group_check=True),
                            waits=avw, inc=tn)
                    else:
                        p.op("tensor", lambda e, kb2=kb2, c02=c02, pk2=pk2, avb=avb: e.matmul(
                            avb[64:128, c02:512], VTA[:, kb2 * 128 + 64:(kb2 + 1) * 128],
                            pk2[0][:, c02:512], start=False, stop=False, skip_group_check=True,
                            tile_position=(0, 64)), waits=avw, inc=tn)
                    p.op("tensor", lambda e, h2=h2, c02=c02, pk2=pk2: e.matmul(
                        pbank[6][64 * h2:64 * h2 + 1, c02:512], ones_col_b, pk2[0][:, c02:512],
                        start=False, stop=False, skip_group_check=True,
                        tile_position=(0, 64 * h2)), inc=tn)
                    last_av[(h2, kb2 % 2)] = tn.n

            pend = None
            for kb in range(kmax + 1):
                j = kb - 4 * g
                c0 = max(j, 0) * 128
                par = kb % 2
                mms = {}
                for h in range(2):
                    sbank = pbank[2 * h + par]
                    wts = []
                    key = (h, par)
                    if key in last_exp:
                        wts.append((sp, last_exp[key]))
                    if g == 0 and kb == 0 and h == 0:
                        wts.append((gd, g_aug))
                    if j < 0:
                        mms[h] = p.op("tensor", lambda e, h=h, kb=kb, c0=c0, g=g, sbank=sbank: e.matmul(
                            sbank[:, c0:512], KA[h][0:65, kb * 128:(kb + 1) * 128],
                            QA[h][0:65, g * 512 + c0:(g + 1) * 512], start=True, stop=True),
                            waits=wts, inc=tn)
                    else:
                        mms[h] = p.op("tensor", lambda e, h=h, kb=kb, c0=c0, g=g, sbank=sbank: e.matmul(
                            sbank[:, c0:512], KA[h][0:64, kb * 128:(kb + 1) * 128],
                            QA[h][0:64, g * 512 + c0:(g + 1) * 512], start=True, stop=False,
                            skip_group_check=True), waits=wts, inc=tn)
                if j >= 0:
                    vms = {}
                    for h in range(2):
                        sbank = pbank[2 * h + par]
                        vms[h] = p.op("vector", lambda e, sbank=sbank, c0=c0: e.tensor_mul(
                            sbank[:, c0:c0 + 128], sbank[:, c0:c0 + 128], triT_f),
                            waits=[(tn, mms[h])], inc=vp)
                    for h in range(2):
                        sbank = pbank[2 * h + par]
                        mms[h] = p.op("tensor", lambda e, h=h, kb=kb, c0=c0, g=g, sbank=sbank: e.matmul(
                            sbank[:, c0:512], KA[h][64:65, kb * 128:(kb + 1) * 128],
                            QA[h][64:65, g * 512 + c0:(g + 1) * 512], start=False, stop=True,
                            skip_group_check=True, tile_position=(64, 0)),
                            waits=[(vp, vms[h])], inc=tn)
                pks = {}
                for h in range(2):
                    sbank = pbank[2 * h + par]
                    pk = PBK[h][par]
                    wts2 = [(tn, mms[h])]
                    if (h, par) in last_av:
                        wts2.append((tn, last_av[(h, par)]))
                    ex = p.op("scalar", lambda e, pk=pk, sbank=sbank, c0=c0: e.activation(
                        pk[:, c0:512], sbank[:, c0:512], AF.Exp, scale=QSCALE),
                        waits=wts2, inc=sp)
                    last_exp[(h, par)] = ex
                    pks[h] = (pk, ex)
                if kb == 0:
                    if pendG is not None:
                        g_tail(*pendG)
                        pendG = None
                    p.op("tensor", lambda e, avb=avb: e.matmul(
                        avb[:, 0:512], onesrb[0:1, 0:128], zrow512b, start=True, stop=False,
                        skip_group_check=True), waits=[(vp, av_free[g % 2])], inc=tn)
                    p.op("tensor", lambda e: e.matmul(
                        pbank[6][:, 0:512], onesrb[0:1, 0:128], zrow512b, start=True, stop=False,
                        skip_group_check=True), waits=[(vp, b6_free)], inc=tn)
                if pend is not None:
                    emit_avz(*pend)
                pend = (kb, c0, pks)
            emit_avz(*pend)
            pendG = (g, avb)
        g_tail(*pendG)
        pendG = None
        prev_q_done = q_done
        prev_v_done = v_done
        prev_hg_exp = sp.n
        prev_hg_tn = tn.n
    v_ctx_done = vp.n

    if debug >= 2:
        for c in range(8):
            p.op("gpsimd", lambda e, c=c: e.dma_start(out=dbg["ctx"][c], in_=ctx_c(c)),
                 waits=[(vp, v_ctx_done)], inc=sod, inc_k=16)
        p.wait("vector", sod, sod.n)
    if debug >= 5:
        sodB = Ch(nc, "sodB")
        p.dma_chs.add(sodB)
        for nm, ap_ in [("KQ", KQ[:, 0:8192]), ("VT", VT[:, 0:4096]),
                        ("PT", PT[:, 0:8448]), ("SM", SM[:, 0:3072]),
                        ("XS", XS[:, 0:4096])]:
            dt_ = F32 if nm in ("SM", "XS") else BF16
            w_ = ap_.shape[-1]
            dbg[nm] = nc.dram_tensor("dbgB_" + nm, [128, w_], dt_, kind="ExternalOutput").ap()
            p.op("gpsimd", lambda e, nm=nm, ap_=ap_: e.dma_start(out=dbg[nm], in_=ap_),
                 waits=[(vp, v_ctx_done), (tn, tn.n), (sp, sp.n)], inc=sodB, inc_k=16)
        p.wait("vector", sodB, sodB.n)
        p.wait("tensor", sodB, sodB.n)
        p.wait("scalar", sodB, sodB.n)
        p.wait("sync", sodB, sodB.n)
        p.wait("gpsimd", sodB, sodB.n)

    p.mark("B_attn0")
    wou_sl = [WS[:, 0:1024].rearrange("p (c e) -> p c e", c=8),
              WS[:, 1024:2048].rearrange("p (c e) -> p c e", c=8)]
    vtv = VT[:].rearrange("p (a t) -> p a t", a=8)
    last_cp = vp.n
    last_wo_used = {0: 0, 1: 0}
    for tg in range(NTG):
        for ech in range(8):
            bank = pbank[ech % 2]
            wsl_o = wou_sl[ech % 2]
            gch = gdA if ech % 2 == 0 else gdB
            p.op("gpsimd", lambda e, ech=ech, wsl_o=wsl_o: e.dma_start(
                out=wsl_o, in_=w_outT.rearrange("(c p) e -> p c e", p=128)[:, :, ech * 128:(ech + 1) * 128]),
                waits=[(sp, prev_q_done if (tg == 0 and ech < 2) else 0),
                       (tn, last_wo_used[ech % 2])], inc=gch, inc_k=16)
            for c in range(8):
                wts = []
                if c == 0:
                    wts = [(gch, 0)]
                    if tg == 0 and ech == 0:
                        wts.append((vp, v_ctx_done))
                    elif tg > 0 or ech >= 2:
                        wts.append((sp, sp.n - 1))
                p.op("tensor", lambda e, c=c, ech=ech, tg=tg, bank=bank, wsl_o=wsl_o: e.matmul(
                    bank[:, 0:TG], wsl_o[:, c, :],
                    ctx_c(c)[:, tg * TG:(tg + 1) * TG],
                    start=(c == 0), stop=(c == 7)), waits=wts, inc=tn)
            last_wo_used[ech % 2] = tn.n
            t_mm = tn.n
            p.op("scalar", lambda e, ech=ech, bank=bank: e.copy(
                vtv[:, ech, :], bank[:, 0:TG]), waits=[(tn, t_mm)] + ([(vp, last_cp)] if ech == 0 else []), inc=sp)
        s_ev = sp.n
        t_all = tn.n
        for ech in range(8):
            last_cp = p.op("vector", lambda e, ech=ech, tg=tg: e.tensor_copy(
                B[:, ech * T + tg * TG: ech * T + (tg + 1) * TG], vtv[:, ech, :]),
                waits=[(sp, s_ev), (tn, t_all)] if ech == 0 else ())
    v_s0_done = vp.n
    p.wait("tensor", vp, v_s0_done)

    p.mark("C0_outproj")
    wg_b = nc.dram_tensor("wg_b", [D, HIDDEN], BF16, kind="Internal").ap()
    wu_b = nc.dram_tensor("wu_b", [D, HIDDEN], BF16, kind="Internal").ap()
    wd_b = nc.dram_tensor("wd_b", [HIDDEN, D], BF16, kind="Internal").ap()
    cst0 = p.op("gpsimd", lambda e: e.dma_start(out=wg_b, in_=wgT), inc=gd, inc_k=16)
    cst1 = p.op("gpsimd", lambda e: e.dma_start(out=wu_b, in_=wuT), inc=gd, inc_k=16)
    cst2 = p.op("gpsimd", lambda e: e.dma_start(out=wd_b, in_=wdT), inc=gd, inc_k=16)
    wsl = [PT[:, 4096:6144].rearrange("p (w c e) -> p w c e", w=2, c=8),
           PT[:, 6144:8192].rearrange("p (w c e) -> p w c e", w=2, c=8)]
    wdbuf = [WS[:, 0:1024], WS[:, 1024:2048]]
    XSB = XS.bitcast(BF16)
    _ht = [XSB[:, i * TG:(i + 1) * TG] for i in range(16)] + \
          [PT[:, i * TG:(i + 1) * TG] for i in range(8)]
    def htile_s(hc):
        return _ht[hc]
    htmp = SM.bitcast(BF16)[:, 4096:4608]

    last_slot_consumed = {0: tn.n, 1: tn.n}
    t_down_prev = 0
    htmp2 = [htmp, SM.bitcast(BF16)[:, 4608:5120]]
    for tg in range(NTG):
        silu_by_par = {}
        mul_by_par = {}
        for hc in range(24):
            slot = hc % 2
            wv = wsl[slot]
            gch = gdA if slot == 0 else gdB
            bA = pbank[2 + 2 * (hc % 2)]
            bB = pbank[3 + 2 * (hc % 2)]
            htm = htmp2[hc % 2]
            p.op("gpsimd", lambda e, hc=hc, wv=wv: e.dma_start(
                out=wv[:, 0], in_=wg_b.rearrange("(c p) e -> p c e", p=128)[:, :, hc * 128:(hc + 1) * 128]),
                waits=[(tn, last_slot_consumed[slot]), (gd, 0)], inc=gch, inc_k=16)
            p.op("gpsimd", lambda e, hc=hc, wv=wv: e.dma_start(
                out=wv[:, 1], in_=wu_b.rearrange("(c p) e -> p c e", p=128)[:, :, hc * 128:(hc + 1) * 128]),
                inc=gch, inc_k=16)
            for c in range(8):
                wts = []
                if c == 0:
                    wts = [(gch, 0)]
                    if hc % 2 in silu_by_par:
                        wts.append((sp, silu_by_par[hc % 2]))
                    else:
                        wts.append((sp, sp.n))
                    if tg == 0 and hc == 0:
                        wts.append((vp, v_s0_done))
                p.op("tensor", lambda e, c=c, tg=tg, wv=wv, bA=bA: e.matmul(
                    bA[:, 0:TG], wv[:, 0, c, :], xn_c(c)[:, tg * TG:(tg + 1) * TG],
                    start=(c == 0), stop=(c == 7)), waits=wts, inc=tn)
            t_gate = tn.n
            for c in range(8):
                wts = []
                if c == 0:
                    wts = [(vp, mul_by_par[hc % 2])] if hc % 2 in mul_by_par else [(vp, vp.n)]
                p.op("tensor", lambda e, c=c, tg=tg, wv=wv, bB=bB: e.matmul(
                    bB[:, 0:TG], wv[:, 1, c, :], xn_c(c)[:, tg * TG:(tg + 1) * TG],
                    start=(c == 0), stop=(c == 7)), waits=wts, inc=tn)
            t_gu = tn.n
            sgt = p.op("scalar", lambda e, bA=bA, htm=htm: e.activation(
                htm, bA[:, 0:TG], AF.Silu, scale=DEQ),
                waits=[(tn, t_gate)] + ([(vp, mul_by_par[hc % 2])] if hc % 2 in mul_by_par else []),
                inc=sp)
            silu_by_par[hc % 2] = sgt
            mul_by_par[hc % 2] = p.op("vector", lambda e, hc=hc, bB=bB, htm=htm: e.tensor_mul(
                htile_s(hc), htm, bB[:, 0:TG]),
                waits=[(sp, sgt), (tn, t_gu), (tn, t_down_prev)], inc=vp)
            last_slot_consumed[slot] = t_gu
        v_h = vp.n
        for hc in range(24):
            slot = hc % 2
            wdb = wdbuf[slot]
            gch = gdA if slot == 0 else gdB
            p.op("gpsimd", lambda e, hc=hc, wdb=wdb: e.dma_start(
                out=wdb, in_=wd_b[hc * 128:(hc + 1) * 128, :]),
                waits=[(tn, last_slot_consumed[slot]), (gd, 0)], inc=gch, inc_k=16)
            for ech in range(8):
                wts = []
                if ech == 0:
                    wts = [(gch, 0), (vp, v_h)]
                    if hc == 0:
                        wts.append((sp, sp.n))
                p.op("tensor", lambda e, hc=hc, ech=ech, wdb=wdb: e.matmul(
                    pbank[ech][:, 0:TG], wdb[:, ech * 128:(ech + 1) * 128],
                    htile_s(hc), start=(hc == 0), stop=(hc == 23)),
                    waits=wts, inc=tn)
            last_slot_consumed[slot] = tn.n
        t_down_prev = tn.n
        for ech in range(8):
            p.op("scalar", lambda e, ech=ech, tg=tg: e.activation(
                xn_c(ech)[:, tg * TG:(tg + 1) * TG], pbank[ech][:, 0:TG],
                AF.Silu, scale=DEQ * DEQ), waits=[(tn, t_down_prev)] if ech == 0 else (), inc=sp)
    s_g0_done = sp.n

    p.mark("C1_ffn")
    vtx = VT[:].rearrange("p (c t) -> p c t", c=8)
    wdkv = WS[:, 0:2048].rearrange("p (c l) -> p c l", c=8)
    ldkv = p.op("gpsimd", lambda e: e.dma_start(
        out=wdkv, in_=WdkvT.rearrange("(c p) l -> p c l", p=128)),
        waits=[(tn, t_down_prev)], inc=gd, inc_k=16)
    tmpC = SM[:, 0:512]
    sqC = SM[:, 512:1024]

    def ln_chunk(xview, dst_view, first_waits, ones_x=None, r3x=None):
        oc = ones_col if ones_x is None else ones_x
        r3l = r3 if r3x is None else r3x
        for c in range(8):
            p.op("tensor", lambda e, c=c, xview=xview, oc=oc: e.matmul(
                pbank[6][0:1, 0:TG], oc, xview[:, c, :], start=(c == 0), stop=(c == 7)),
                waits=first_waits if c == 0 else (), inc=tn)
        t_sum2 = tn.n
        smm = {}
        for c in range(8):
            sv = p.op("scalar", lambda e, c=c, xview=xview: e.square(sqC, xview[:, c, :]),
                      waits=[(tn, smm[c - 1])] if c >= 1 else [(tn, t_sum2)], inc=sp)
            smm[c] = p.op("tensor", lambda e, c=c: e.matmul(
                pbank[7][0:1, 0:TG], ones_col, sqC, start=(c == 0), stop=(c == 7)),
                waits=[(sp, sv)], inc=tn)
        t_sq2 = tn.n
        p.op("vector", lambda e: e.tensor_scalar(r1, pbank[6][0:1, 0:TG], 1.0 / D, None, ALU.mult),
             waits=[(tn, t_sq2)])
        p.op("vector", lambda e: e.tensor_scalar(r2, pbank[7][0:1, 0:TG], 1.0 / D, None, ALU.mult))
        p.op("vector", lambda e: e.tensor_tensor(r3l, r1, r1, ALU.mult))
        p.op("vector", lambda e: e.tensor_tensor(r2, r2, r3l, ALU.subtract))
        p.op("vector", lambda e: e.tensor_scalar(r2, r2, EPS_LN, None, ALU.add))
        v_st = vp.n
        p.op("scalar", lambda e: e.sqrt(r2, r2), waits=[(vp, v_st)])
        s_sq = sp.n
        p.op("vector", lambda e: e.reciprocal(r2, r2), waits=[(sp, s_sq)])
        p.op("vector", lambda e: e.tensor_tensor(r1, r1, r2, ALU.mult))
        p.op("vector", lambda e: e.tensor_scalar(r1, r1, -1.0, None, ALU.mult))
        v_ab2 = vp.n
        p.op("tensor", lambda e: e.matmul(pbank[4][:, 0:TG], ones_row, r2, start=True, stop=True),
             waits=[(vp, v_ab2)], inc=tn)
        p.op("tensor", lambda e: e.matmul(pbank[5][:, 0:TG], ones_row, r1, start=True, stop=True),
             inc=tn)
        t_bc2 = tn.n
        for c in range(8):
            p.op("vector", lambda e, c=c, xview=xview: e.tensor_tensor(
                tmpC, xview[:, c, :], pbank[4][:, 0:TG], ALU.mult),
                waits=[(tn, t_bc2)] if c == 0 else ())
            p.op("vector", lambda e, c=c, dst_view=dst_view: e.tensor_tensor(
                dst_view[:, c, :], tmpC, pbank[5][:, 0:TG], ALU.add))
        return vp.n

    for tg in range(NTG):
        xsv = XS[:].rearrange("p (c t) -> p c t", c=8)
        src = xT.rearrange("(c p) t -> p c t", p=128)[:, :, tg * TG:(tg + 1) * TG]
        wts = [(sp, s_g0_done)] if tg == 0 else [(tn, tn.n)]
        ldx = p.op("sync", lambda e, s=src, d=xsv: e.dma_start(out=d, in_=s),
                   waits=wts + [(vp, vp.n)], inc=ld, inc_k=16)
        p.wait("vector", ld, ldx)
        for c in range(8):
            p.op("vector", lambda e, c=c, tg=tg: e.tensor_mul(
                tmpC, B[:, c * T + tg * TG: c * T + (tg + 1) * TG],
                xn_c(c)[:, tg * TG:(tg + 1) * TG]))
            p.op("vector", lambda e, c=c, xsv=xsv: e.tensor_add(
                xsv[:, c, :], xsv[:, c, :], tmpC))
        v_x1 = vp.n
        ln_chunk(xsv, vtx, [(vp, v_x1)])
        v_xn1 = vp.n
        for sub in range(4):
            bank = pbank[2 + sub % 2]
            for c in range(8):
                p.op("tensor", lambda e, c=c, sub=sub, bank=bank: e.matmul(
                    bank[:, 0:LAT], vtx[:, c, sub * 128:(sub + 1) * 128], wdkv[:, c, :],
                    start=(c == 0), stop=(c == 7)),
                    waits=[(vp, v_xn1), (gd, ldkv), (sp, sp.n)] if c == 0 else (), inc=tn)
            t_ck = tn.n
            p.op("scalar", lambda e, tg=tg, sub=sub, bank=bank: e.copy(
                PT[:, (tg * 4 + sub) * LAT:(tg * 4 + sub + 1) * LAT], bank[:, 0:LAT]),
                waits=[(tn, t_ck)], inc=sp)
        xsv4 = XS[:].rearrange("p (c b q) -> p c b q", c=8, b=4)
        for c in range(8):
            p.op("vector", lambda e, c=c, xsv4=xsv4: e.tensor_tensor(
                tmpC[:, 0:256].rearrange("p (b q) -> p b q", b=2),
                xsv4[:, c, 1:4:2, :], xsv4[:, c, 0:4:2, :], ALU.subtract))
            p.op("vector", lambda e, c=c, tg=tg, xsv4=xsv4: e.scalar_tensor_tensor(
                A[:, c * T + tg * TG: c * T + tg * TG + 256].rearrange("p (b q) -> p b q", b=2),
                tmpC[:, 0:256].rearrange("p (b q) -> p b q", b=2),
                u_col[:, 0:1], xsv4[:, c, 0:4:2, :], ALU.mult, ALU.add))
    s_ckv_done = sp.n
    for c in range(8):
        for tg in range(NTG):
            p.op("vector", lambda e, c=c, tg=tg: e.tensor_copy(
                A[:, c * 2048 + tg * 256: c * 2048 + (tg + 1) * 256],
                A[:, c * T + tg * TG: c * T + tg * TG + 256]))
    v_c2_done = vp.n

    if debug >= 3:
        for c in range(8):
            p.op("gpsimd", lambda e, c=c: e.dma_start(
                out=dbg["x1o"][c], in_=A[:, c * 2048:(c + 1) * 2048]),
                waits=[(vp, v_c2_done), (sp, s_ckv_done)], inc=sod, inc_k=16)
        p.op("gpsimd", lambda e: e.dma_start(
            out=dbg["ckv"].rearrange("t p l -> p t l"),
            in_=PT[:, 0:32 * LAT].rearrange("p (t l) -> p t l", t=32)),
            waits=[(sp, s_ckv_done)], inc=sod, inc_k=16)
        p.wait("vector", sod, sod.n)
        p.wait("sync", sod, sod.n)
        p.wait("tensor", sod, sod.n)

    p.mark("C2_epi")
    for t_ in range(32):
        for l_ in range(2):
            p.op("sync", lambda e, t_=t_, l_=l_: e.dma_start_transpose(
                KQ[:, l_ * T + t_ * 128: l_ * T + (t_ + 1) * 128],
                PT[:, t_ * LAT + l_ * 128: t_ * LAT + (l_ + 1) * 128]),
                waits=[(sp, s_ckv_done), (vp, v_c2_done), (tn, tn.n)] if (t_ == 0 and l_ == 0) else (),
                inc=tsp, inc_k=16)
    t_ckvT = tsp.n

    def ckvT_c(l_):
        return KQ[:, l_ * T:(l_ + 1) * T]

    vt_wuv = VT[:, 0:2048].rearrange("p (c l) -> p c l", c=8)
    vt_wuk = VT[:, 2048:4096].rearrange("p (c l) -> p c l", c=8)
    ws_wuq = WS[:, 0:2048].rearrange("p (c l) -> p c l", c=8)
    l1 = p.op("gpsimd", lambda e: e.dma_start(
        out=vt_wuv, in_=Wuv.rearrange("(c p) l -> p c l", p=128)),
        waits=[(vp, v_c2_done), (sp, s_ckv_done)], inc=gd, inc_k=16)
    l2 = p.op("gpsimd", lambda e: e.dma_start(
        out=vt_wuk, in_=Wuk.rearrange("(c p) l -> p c l", p=128)), inc=gd, inc_k=16)
    l3 = p.op("gpsimd", lambda e: e.dma_start(
        out=ws_wuq, in_=Wuq.rearrange("(c p) l -> p c l", p=128)),
        waits=[(sp, s_ckv_done)], inc=gd, inc_k=16)
    for am in range(2):
        for c in range(8):
            p.op("tensor", lambda e, am=am, c=c: e.matmul(
                pbank[am][:, 0:LAT], ws_wuq[:, c, am * 128:(am + 1) * 128], vt_wuk[:, c, :],
                start=(c == 0), stop=(c == 7)),
                waits=[(gd, l3), (sp, sp.n)] if (am == 0 and c == 0) else (), inc=tn)
    t_inner = tn.n
    ws_inner = WS[:, 0:512].rearrange("p (a l) -> p a l", a=2)
    for am in range(2):
        p.op("scalar", lambda e, am=am: e.copy(ws_inner[:, am, :], pbank[am][:, 0:LAT]),
             waits=[(tn, t_inner)] if am == 0 else (), inc=sp)
    s_inner = sp.n
    vt_wo = VT[:, 2048:4096].rearrange("p (c n) -> p c n", c=8)
    FA = A[:, 18432:20480].rearrange("p (l d) -> p l d", l=2)
    t_prev_f = t_inner
    for dg in range(4):
        lwo = p.op("gpsimd", lambda e, dg=dg: e.dma_start(
            out=vt_wo, in_=WoT.rearrange("(c p) d -> p c d", p=128)[:, :, dg * 256:(dg + 1) * 256]),
            waits=[(tn, t_prev_f)], inc=gdA, inc_k=16)
        for lm in range(2):
            for c in range(8):
                p.op("tensor", lambda e, lm=lm, c=c, dg=dg: e.matmul(
                    pbank[2 + lm][:, 0:256], vt_wuv[:, c, lm * 128:(lm + 1) * 128], vt_wo[:, c, :],
                    start=(c == 0), stop=(c == 7)),
                    waits=[(gdA, 0), (sp, sp.n)] if (lm == 0 and c == 0) else (), inc=tn)
        t_f = tn.n
        t_prev_f = t_f
        for lm in range(2):
            p.op("scalar", lambda e, lm=lm, dg=dg: e.copy(
                FA[:, lm, dg * 256:(dg + 1) * 256], pbank[2 + lm][:, 0:256]),
                waits=[(tn, t_f)] if lm == 0 else (), inc=sp)
    s_F = sp.n
    vt_wdq = VT[:, 0:2048].rearrange("p (a v) -> p a v", a=2)
    lwdq = p.op("gpsimd", lambda e: e.dma_start(
        out=vt_wdq, in_=Wdq.rearrange("(a p) v -> p a v", p=128)),
        waits=[(tn, t_f)], inc=gd, inc_k=16)
    ETA = A[:, 16384:18432].rearrange("p (l v) -> p l v", l=2)
    for lm in range(2):
        for ng in range(2):
            for ac in range(2):
                p.op("tensor", lambda e, lm=lm, ng=ng, ac=ac: e.matmul(
                    pbank[4 + lm][:, 0:TG], ws_inner[:, ac, lm * 128:(lm + 1) * 128],
                    vt_wdq[:, ac, ng * TG:(ng + 1) * TG],
                    start=(ac == 0), stop=(ac == 1)),
                    waits=([(gd, lwdq), (sp, s_F)] if (lm == 0 and ng == 0 and ac == 0) else
                           ([(sp, sp.n)] if ac == 0 else [])), inc=tn)
            t_et = tn.n
            p.op("scalar", lambda e, lm=lm, ng=ng: e.copy(
                ETA[:, lm, ng * TG:(ng + 1) * TG], pbank[4 + lm][:, 0:TG]),
                waits=[(tn, t_et)], inc=sp)
    s_ET = sp.n

    ws_wdqT = WS[:, 0:2048].rearrange("p (c a) -> p c a", c=8)
    lq1 = p.op("gpsimd", lambda e: e.dma_start(
        out=ws_wdqT, in_=WdqT.rearrange("(c p) a -> p c a", p=128)),
        waits=[(sp, s_ET), (tn, tn.n)], inc=gd, inc_k=16)
    q1A = A[:, 20480:24576].rearrange("p (a t) -> p a t", a=2)
    kqv = A[:, 0:16384].rearrange("p (c t) -> p c t", c=8)
    for tgo in range(4):
        x1v = kqv[:, :, tgo * TG:(tgo + 1) * TG]
        ln_chunk(x1v, vtx, [(vp, vp.n), (sp, sp.n)], ones_x=ones_col_b)
        v_xo = vp.n
        for am in range(2):
            bank = pbank[2 + am]
            for c in range(8):
                p.op("tensor", lambda e, am=am, c=c, bank=bank: e.matmul(
                    bank[:, 0:TG], ws_wdqT[:, c, am * 128:(am + 1) * 128], vtx[:, c, :].rearrange("p a t -> p (a t)") if False else vtx[:, c, :],
                    start=(c == 0), stop=(c == 7)),
                    waits=[(gd, lq1), (vp, v_xo), (sp, sp.n)] if c == 0 else (), inc=tn)
            t_q1 = tn.n
            p.op("scalar", lambda e, am=am, tgo=tgo, bank=bank: e.copy(
                q1A[:, am, tgo * TG:(tgo + 1) * TG], bank[:, 0:TG]),
                waits=[(tn, t_q1)], inc=sp)
    s_q1 = sp.n
    ws_wuqT = WS[:, 0:2048].rearrange("p (a v) -> p a v", a=2)
    lq2 = p.op("gpsimd", lambda e: e.dma_start(
        out=ws_wuqT, in_=WuqT.rearrange("(a p) v -> p a v", p=128)),
        waits=[(tn, tn.n)], inc=gd, inc_k=16)
    for tgo in range(4):
        for vm in range(8):
            bank = pbank[vm % 2]
            for ac in range(2):
                p.op("tensor", lambda e, vm=vm, ac=ac, tgo=tgo, bank=bank: e.matmul(
                    bank[:, 0:TG], ws_wuqT[:, ac, vm * 128:(vm + 1) * 128],
                    q1A[:, ac, tgo * TG:(tgo + 1) * TG],
                    start=(ac == 0), stop=(ac == 1)),
                    waits=[(gd, lq2), (sp, s_q1)] if (tgo == 0 and vm == 0 and ac == 0) else
                          ([(sp, sp.n - 1)] if (ac == 0 and (vm >= 2 or tgo > 0)) else []), inc=tn)
            t_q = tn.n
            p.op("scalar", lambda e, vm=vm, tgo=tgo, bank=bank: e.copy(
                B[:, vm * 2048 + tgo * TG: vm * 2048 + (tgo + 1) * TG], bank[:, 0:TG]),
                waits=[(tn, t_q)], inc=sp)
    s_q = sp.n

    if debug >= 3:
        for c in range(8):
            p.op("gpsimd", lambda e, c=c: e.dma_start(
                out=dbg["q"][c], in_=B[:, c * 2048:(c + 1) * 2048]),
                waits=[(sp, s_q)], inc=sod, inc_k=16)
        p.wait("vector", sod, sod.n)
        p.wait("tensor", sod, sod.n)

    K1A = A[:, 20480:24576]
    ZtL = SM[:, 192:256]
    ZrL = SM[:, 384:448]
    zpL = [SM[:, 448:449], SM[:, 449:450]]
    zrow512 = SM[0:1, 2176:2688]
    zrowDb = SM.bitcast(BF16)[0:1, 4352:5376]
    ctxs = SM.bitcast(BF16)[:, 5376:5888]
    zrow_m = p.op("vector", lambda e: e.memset(zrow512, 0.0), waits=[(vp, vp.n)])
    last_exp = {}
    last_zadd2 = {}
    prev_j_done = vp.n
    for hgl in range(8):
        for kgp in range(8):
            for lm in range(2):
                p.op("tensor", lambda e, kgp=kgp, lm=lm, hgl=hgl: e.matmul(
                    pbank[6 + kgp % 2][:, 0:TG], ETA[:, lm, hgl * 128:(hgl + 1) * 128],
                    ckvT_c(lm)[:, kgp * TG:(kgp + 1) * TG],
                    start=(lm == 0), stop=(lm == 1)),
                    waits=([(tsp, t_ckvT), (sp, s_q)] if (kgp == 0 and lm == 0 and hgl == 0) else
                           ([(sp, sp.n - 1), (vp, prev_j_done)] if (lm == 0 and (kgp >= 2 or hgl > 0)) else
                            ([(vp, prev_j_done)] if lm == 0 else []))), inc=tn)
            t_k1 = tn.n
            p.op("scalar", lambda e, kgp=kgp: e.copy(
                K1A[:, kgp * TG:(kgp + 1) * TG], pbank[6 + kgp % 2][:, 0:TG]),
                waits=[(tn, t_k1)], inc=sp)
        s_k1 = sp.n
        def d_tail(u):
            nonlocal prev_j_done
            j2, nk2, v_zr2 = u
            p.op("tensor", lambda e: e.matmul(
                pbank[4][:, 0:512], zrowDb[0:1, 0:128], zrowDb[0:1, 0:512], start=True, stop=False,
                skip_group_check=True),
                waits=[(tspA, tspA.n), (tspB, tspB.n), (vp, prev_j_done)], inc=tn)
            for kb in range(nk2):
                for l_ in range(2):
                    for h in range(2):
                        p.op("tensor", lambda e, kb=kb, l_=l_, h=h: e.matmul(
                            pbank[4][:, (2 * h + l_) * 128:(2 * h + l_ + 1) * 128],
                            PT[:, kb * LAT + l_ * 128: kb * LAT + (l_ + 1) * 128],
                            XSB[:, h * T + kb * 128: h * T + (kb + 1) * 128],
                            start=False, stop=False,
                            skip_group_check=True), inc=tn)
            p.op("tensor", lambda e: e.matmul(
                pbank[4][:, 0:512], zrowDb[0:1, 0:128], zrowDb[0:1, 0:512], start=False, stop=True,
                skip_group_check=True), inc=tn)
            t_av2 = tn.n
            b5v = prev_j_done
            for h in range(2):
                tz = p.op("tensor", lambda e, j2=j2, h=h: e.transpose(
                    pbank[5][0:1, 0:128], ZrL[:, 2 * j2 + h:2 * j2 + h + 1], ident_f),
                    waits=[(vp, v_zr2), (vp, b5v)], inc=tn)
                ez = p.op("vector", lambda e: e.tensor_copy(zr2[0:1, :], pbank[5][0:1, 0:128]),
                          waits=[(tn, tz)], inc=vp)
                tb = p.op("tensor", lambda e: e.matmul(
                    pbank[5][:, 128:256], ones_row, zr2[0:1, :], start=True, stop=True),
                    waits=[(vp, ez)], inc=tn)
                cz = p.op("vector", lambda e: e.tensor_copy(zbc, pbank[5][:, 128:256]),
                          waits=[(tn, tb)], inc=vp)
                for l_ in range(2):
                    p.op("vector", lambda e, h=h, l_=l_: e.tensor_mul(
                        ctxs[:, (2 * h + l_) * 128:(2 * h + l_ + 1) * 128],
                        pbank[4][:, (2 * h + l_) * 128:(2 * h + l_ + 1) * 128], zbc),
                        waits=[(tn, t_av2)] if l_ == 0 else ())
                b5v = vp.n
            v_ctx2 = vp.n
            p.op("tensor", lambda e: e.matmul(
                pbank[6][:, 0:128], zrowDb[0:1, 0:128], zrowDb[0:1, 0:128], start=True, stop=False,
                skip_group_check=True), waits=[(vp, v_ctx2)], inc=tn)
            for h in range(2):
                h_glob2 = hgl * 2 + h
                for l_ in range(2):
                    p.op("tensor", lambda e, h=h, l_=l_, h_glob2=h_glob2: e.matmul(
                        pbank[6][64 * h:64 * h + 64, 0:128],
                        FA[:, l_, h_glob2 * 64:(h_glob2 + 1) * 64],
                        ctxs[:, (2 * h + l_) * 128:(2 * h + l_ + 1) * 128],
                        start=False, stop=False,
                        skip_group_check=True), inc=tn)
            p.op("tensor", lambda e: e.matmul(
                pbank[6][:, 0:128], zrowDb[0:1, 0:128], zrowDb[0:1, 0:128], start=False, stop=True,
                skip_group_check=True), inc=tn)
            t_y = tn.n
            prev_j_done = p.op("vector", lambda e, hgl=hgl, j2=j2: e.tensor_copy(
                B[:, 16384 + hgl * 2048 + j2 * 128: 16384 + hgl * 2048 + (j2 + 1) * 128],
                pbank[6][:, 0:128]), waits=[(tn, t_y)], inc=vp)

        pendD = None
        for j in range(NBO):
            nk = 2 * j + 2
            s_prs = {}
            for h in range(2):
                q_sl = B[64 * h:64 * h + 64, hgl * 2048 + j * 128: hgl * 2048 + (j + 1) * 128]
                zcol = ZtL[:, 2 * j + h:2 * j + h + 1]
                p.op("vector", lambda e, zcol=zcol: e.memset(zcol, 0.0))
                nkg = (nk + 3) // 4
                tspX = tspA if h == 0 else tspB
                for kg in range(nkg):
                    c0 = kg * 512
                    c1 = min(nk * 128, c0 + 512)
                    W = c1 - c0
                    bank = pbank[2 * h + kg % 2]
                    wts = [(sp, s_k1)]
                    key = (h, kg % 2)
                    if key in last_exp:
                        wts.append((sp, last_exp[key]))
                    mm = p.op("tensor", lambda e, q_sl=q_sl, c0=c0, c1=c1, bank=bank, h=h, W=W: e.matmul(
                        bank[:, 0:W], q_sl, K1A[64 * h:64 * h + 64, c0:c1],
                        start=True, stop=True), waits=wts, inc=tn)
                    vmask = None
                    for rel, mk in ((nk - 2, mkA_f), (nk - 1, mkB_f)):
                        boff = rel * 128
                        if c0 <= boff < c1:
                            vmask = p.op("vector", lambda e, bank=bank, boff=boff, c0=c0, mk=mk: e.tensor_add(
                                bank[:, boff - c0:boff - c0 + 128],
                                bank[:, boff - c0:boff - c0 + 128], mk),
                                waits=[(tn, mm)], inc=vp)
                    zp = zpL[kg % 2]
                    wts2 = [(vp, vmask)] if vmask else [(tn, mm)]
                    if kg == 0:
                        wts2.append((tspX, tspX.n))
                    if (kg % 2) in last_zadd2:
                        wts2.append((vp, last_zadd2[kg % 2]))
                    pdst = VT[:, 0:T] if h == 0 else A[:, 24576:24576 + T]
                    sx = p.op("scalar", lambda e, bank=bank, W=W, c0=c0, zp=zp, pdst=pdst: e.activation(
                        pdst[:, c0:c0 + W], bank[:, 0:W], AF.Exp,
                        bias=negml1_col[:, 0:1], scale=QSCALE, accum_out=zp), waits=wts2, inc=sp)
                    last_exp[(h, kg % 2)] = sx
                    last_zadd2[kg % 2] = p.op(
                        "vector", lambda e, zcol=zcol, zp=zp: e.tensor_add(zcol, zcol, zp),
                        waits=[(sp, sx)], inc=vp)
                p.op("vector", lambda e, zcol=zcol, j=j, h=h: e.reciprocal(
                    ZrL[:, 2 * j + h:2 * j + h + 1], zcol))
                s_prs[h] = sp.n
            v_zr2 = vp.n
            if pendD is not None:
                d_tail(pendD)
            t_avx = tn.n
            for h in range(2):
                tspX = tspA if h == 0 else tspB
                psrc = VT[:, 0:T] if h == 0 else A[:, 24576:24576 + T]
                p.op("sync", lambda e, h=h, nk=nk, psrc=psrc: e.dma_start_transpose(
                    XSB[:, h * T: h * T + nk * 128].rearrange("p (t q) -> p t q", t=nk),
                    psrc[:, 0:nk * 128]),
                    waits=[(sp, s_prs[h]), (tn, t_avx)], inc=tspX, inc_k=16)
            pendD = (j, nk, v_zr2)
        d_tail(pendD)
    v_y_done = vp.n

    if debug >= 3:
        for c in range(8):
            p.op("gpsimd", lambda e, c=c: e.dma_start(
                out=dbg["y"][c], in_=B[:, 16384 + c * 2048: 16384 + (c + 1) * 2048]),
                waits=[(vp, v_y_done)], inc=sod, inc_k=16)
        p.wait("vector", sod, sod.n)

    p.mark("D_attn1")
    wgm = CSTB[:, 136:200].rearrange("p (c e) -> p c e", c=8)
    lwg = p.op("gpsimd", lambda e: e.dma_start(
        out=wgm, in_=WgT_moe.rearrange("(c p) e -> p c e", p=128)),
        waits=[(sp, s_q)], inc=gd, inc_k=16)
    eps_r = CST[0:1, 776:784]
    lep = p.op("gpsimd", lambda e: e.dma_start(out=eps_r, in_=eps_row_in),
               waits=[(sp, s_q)], inc=gd, inc_k=16)
    ws_be = WS[0:8, 0:1024]
    lbe = p.op("gpsimd", lambda e: e.dma_start(out=ws_be, in_=be_in),
               waits=[(sp, s_q)], inc=gd, inc_k=16)
    wet2 = [WS[:, 1024:2048].rearrange("p (c o) -> p c o", c=8),
            KQ[:, 0:1024].rearrange("p (c o) -> p c o", c=8)]
    epsbc = SM[:, 2592:2600]
    logi = SM[:, 2600:2608]
    lp1 = SM[:, 2608:2616]
    mask1 = SM[:, 2616:2624]
    mcol1 = SM[:, 2624:2625]
    mcol2 = SM[:, 2625:2626]
    e2c = SM[:, 2626:2627]
    w1c = SM[:, 2627:2628]
    w2c = SM[:, 2628:2629]
    r3E = SM[0:1, 0:512]
    crows = SM.bitcast(BF16)[0:8, 5264:5776]
    crow1 = r1
    x2c = SM[:, 512:1024]
    zrowEb = SM.bitcast(BF16)[0:1, 5776:6032]
    p.op("vector", lambda e: e.memset(zrowEb, 0.0), waits=[(vp, vp.n)], inc=vp)
    tb_e = p.op("tensor", lambda e: e.matmul(
        pbank[7][:, 0:8], ones_row, eps_r, start=True, stop=True),
        waits=[(gd, lep), (vp, v_y_done)], inc=tn)
    p.op("vector", lambda e: e.tensor_copy(epsbc, pbank[7][:, 0:8]), waits=[(tn, tb_e)])

    mm_hist = []
    for tgo in range(4):
        x1v = kqv[:, :, tgo * TG:(tgo + 1) * TG]
        v_xn1o = ln_chunk(x1v, vtx, [(vp, vp.n), (sp, sp.n), (tn, tn.n), (so, 0)], ones_x=ones_col_b, r3x=r3E)
        for sub in range(4):
            ccol = SM[:, 2560 + sub * 8: 2568 + sub * 8]
            for c in range(8):
                p.op("tensor", lambda e, c=c, sub=sub: e.matmul(
                    pbank[6][:, 0:8], vtx[:, c, sub * 128:(sub + 1) * 128], wgm[:, c, :],
                    start=(c == 0), stop=(c == 7)),
                    waits=[(gd, lwg), (vp, v_xn1o), (vp, vp.n)] if c == 0 else (), inc=tn)
            t_lg = tn.n
            p.op("vector", lambda e: e.tensor_copy(logi, pbank[6][:, 0:8]), waits=[(tn, t_lg)])
            p.op("vector", lambda e: e.tensor_add(lp1, logi, epsbc))
            p.op("vector", lambda e: e.reduce_max(mcol1, lp1, axis=AX.X))
            p.op("vector", lambda e: e.tensor_scalar(mask1, lp1, mcol1[:, 0:1], None, ALU.is_ge))
            p.op("vector", lambda e: e.scalar_tensor_tensor(
                lp1, mask1, -1.0e9, lp1, ALU.mult, ALU.add))
            p.op("vector", lambda e: e.reduce_max(mcol2, lp1, axis=AX.X))
            p.op("vector", lambda e, ccol=ccol: e.tensor_scalar(ccol, lp1, mcol2[:, 0:1], None, ALU.is_ge))
            p.op("vector", lambda e: e.reduce_max(mcol1, logi, axis=AX.X))
            p.op("vector", lambda e: e.scalar_tensor_tensor(
                lp1, mask1, -1.0e9, logi, ALU.mult, ALU.add))
            p.op("vector", lambda e: e.reduce_max(mcol2, lp1, axis=AX.X))
            p.op("vector", lambda e: e.tensor_tensor(e2c, mcol2, mcol1, ALU.subtract))
            v_pre = vp.n
            p.op("scalar", lambda e: e.activation(e2c, e2c, AF.Exp), waits=[(vp, v_pre)])
            s_e2 = sp.n
            p.op("vector", lambda e: e.tensor_scalar(w1c, e2c, 1.0, None, ALU.add),
                 waits=[(sp, s_e2)])
            p.op("vector", lambda e: e.reciprocal(w1c, w1c))
            p.op("vector", lambda e: e.tensor_tensor(w2c, e2c, w1c, ALU.mult))
            p.op("vector", lambda e: e.tensor_scalar(mask1, mask1, w1c[:, 0:1], None, ALU.mult))
            p.op("vector", lambda e, ccol=ccol: e.scalar_tensor_tensor(
                ccol, ccol, w2c[:, 0:1], mask1, ALU.mult, ALU.add))
            v_coef = vp.n
            tc = p.op("tensor", lambda e, ccol=ccol: e.transpose(
                pbank[7][0:8, 0:128], ccol, ident_f), waits=[(vp, v_coef)], inc=tn)
            p.op("vector", lambda e, sub=sub: e.tensor_copy(
                crows[:, sub * 128:(sub + 1) * 128], pbank[7][0:8, 0:128]),
                waits=[(tn, tc)])
        v_crows = vp.n
        rhse2 = [XSB[:, 0:2048].rearrange("p (c t) -> p c t", c=8),
                 XSB[:, 2048:4096].rearrange("p (c t) -> p c t", c=8)]
        g1v = XSB[:, 4096:8192].rearrange("p (c t) -> p c t", c=8)
        for tp in range(2):
            toff = tp * 256
            for och in range(8):
                p.op("tensor", lambda e, och=och, toff=toff: e.matmul(
                    pbank[och % 4][:, (och // 4) * 256:(och // 4) * 256 + 256],
                    ws_be[:, och * 128:(och + 1) * 128], crows[:, toff:toff + 256],
                    start=(och < 4), stop=False, skip_group_check=True),
                    waits=[(gd, lbe), (vp, v_crows), (sp, sp.n)] if och == 0 else (), inc=tn)
            for e_ in range(E):
                rhse = rhse2[e_ % 2]
                tmpCe = tmpC[:, 0:256] if e_ % 2 == 0 else tmpC[:, 256:512]
                for si in range(2):
                    sub = tp * 2 + si
                    ccol = SM[:, 2560 + sub * 8: 2568 + sub * 8]
                    p.op("tensor", lambda e, e_=e_, si=si, ccol=ccol: e.matmul(
                        pbank[7][0:1, si * 128:(si + 1) * 128], ccol[:, e_:e_ + 1], ident_f,
                        start=(si == 0), stop=(si == 1)),
                        waits=[(vp, vp.n)] if si == 0 else (), inc=tn)
                t_cr = tn.n
                vr = p.op("vector", lambda e: e.tensor_copy(crow1[0:1, 0:256], pbank[7][0:1, 0:256]),
                          waits=[(tn, t_cr)], inc=vp)
                tbx = p.op("tensor", lambda e: e.matmul(
                    pbank[7][:, 256:512], ones_row, crow1[0:1, 0:256], start=True, stop=True),
                    waits=[(vp, vr)], inc=tn)
                vb = p.op("vector", lambda e, tmpCe=tmpCe: e.tensor_copy(tmpCe, pbank[7][:, 256:512]),
                          waits=[(tn, tbx)], inc=vp)
                for c in range(8):
                    w_m = []
                    if c == 0:
                        w_m = [(tn, mm_hist[-9])] if len(mm_hist) >= 9 else [(tn, tn.n)]
                    p.op("vector", lambda e, c=c, toff=toff, rhse=rhse, tmpCe=tmpCe: e.tensor_mul(
                        rhse[:, c, :], vtx[:, c, toff:toff + 256], tmpCe),
                        waits=w_m)
                v_rh = vp.n
                for och in range(8):
                    wet = wet2[och % 2]
                    chn = gdA if och % 2 == 0 else gdB
                    if len(mm_hist) >= 2:
                        w_l = [(tn, mm_hist[-2])]
                    else:
                        w_l = [(tn, tn.n)]
                    lwe = p.op("gpsimd", lambda e, och=och, e_=e_, wet=wet: e.dma_start(
                        out=wet[:, :, 0:128],
                        in_=WeT[e_].rearrange("(c p) o -> p c o", p=128)[:, :, och * 128:(och + 1) * 128]),
                        waits=w_l, inc=chn, inc_k=16)
                    for c in range(8):
                        p.op("tensor", lambda e, c=c, och=och, toff=toff, wet=wet, rhse=rhse: e.matmul(
                            pbank[och % 4][:, (och // 4) * 256:(och // 4) * 256 + 256],
                            wet[:, c, 0:128], rhse[:, c, :],
                            start=False, stop=False, skip_group_check=True),
                            waits=[(chn, lwe), (vp, v_rh)] if c == 0 else (), inc=tn)
                    mm_hist.append(tn.n)
            for och in range(8):
                p.op("tensor", lambda e, och=och: e.matmul(
                    pbank[och % 4][:, (och // 4) * 256:(och // 4) * 256 + 256],
                    zrowEb[0:1, 0:128], zrowEb[0:1, 0:256], start=False, stop=True,
                    skip_group_check=True), inc=tn)
            t_moe = tn.n
            for och in range(8):
                p.op("scalar", lambda e, och=och, toff=toff: e.activation(
                    g1v[:, och, toff:toff + 256],
                    pbank[och % 4][:, (och // 4) * 256:(och // 4) * 256 + 256],
                    AF.Silu), waits=[(tn, t_moe)] if och == 0 else (), inc=sp)
        s_g1 = sp.n
        if debug >= 4:
            for c in range(8):
                p.op("gpsimd", lambda e, c=c, tgo=tgo: e.dma_start(
                    out=dbg["g1"][c][:, tgo * TG:(tgo + 1) * TG], in_=g1v[:, c, :]),
                    waits=[(sp, s_g1)], inc=sod, inc_k=16)
            p.wait("scalar", sod, sod.n)
            p.wait("vector", sod, sod.n)
        for c in range(8):
            p.op("vector", lambda e, c=c, tgo=tgo: e.tensor_mul(
                tmpA, B[:, 16384 + c * 2048 + tgo * TG: 16384 + c * 2048 + (tgo + 1) * TG],
                g1v[:, c, :]), waits=[(sp, s_g1)] if c == 0 else ())
            p.op("vector", lambda e, c=c, tgo=tgo: e.tensor_add(
                x2c, kqv[:, c, tgo * TG:(tgo + 1) * TG], tmpA))
            v_x2 = vp.n
            p.op("sync", lambda e, c=c, tgo=tgo: e.dma_start(
                out=out.rearrange("(c p) t -> p c t", p=128)[:, c, tgo * TG:(tgo + 1) * TG],
                in_=x2c), waits=[(vp, v_x2)], inc=so, inc_k=16)
            p.wait("vector", so, so.n)

    def finish(p):
        p.wait("sync", so, so.n)
        p.wait("gpsimd", sod, sod.n)

    p.mark("E_moe")
    return nc, p, prog_tensors, finish


def _close(nc, p, prog_tensors, finish):
    finish(p)
    with nc.Block() as block:
        block.tensor(lambda e: [f(e) for f in p.ops["tensor"]])
        block.vector(lambda e: [f(e) for f in p.ops["vector"]])
        block.scalar(lambda e: [f(e) for f in p.ops["scalar"]])
        block.sync(lambda e: [f(e) for f in p.ops["sync"]])
        block.gpsimd(lambda e: [f(e) for f in p.ops["gpsimd"]])
    for cm in reversed(prog_tensors):
        cm.__exit__(None, None, None)


def run_prog(nc, p, prog_tensors, finish, in_map, cores=1):
    from concourse.bass_utils import run_bass_kernel_spmd
    _close(nc, p, prog_tensors, finish)
    maps = [in_map] * cores if isinstance(in_map, dict) else in_map
    return run_bass_kernel_spmd(nc, maps, core_ids=list(range(cores)))



_CACHE = {}


def kernel(L0_attn_x, L0_attn_w_in, L0_attn_w_out, L0_ffn_w_gate, L0_ffn_w_up,
           L0_ffn_w_down, L1_attn_W_dq, L1_attn_W_uq, L1_attn_W_dkv,
           L1_attn_W_uk, L1_attn_W_uv, L1_attn_W_o, L1_ffn_Wg, L1_ffn_We,
           L1_ffn_be):
    from concourse.bass_utils import run_bass_kernel_spmd

    x = np.asarray(L0_attn_x)
    B_, T_, D_ = x.shape
    w_in = np.ascontiguousarray(L0_attn_w_in)
    tri128 = np.tril(np.ones((128, 128), np.float32))
    NEG = np.float32(NEG_BIG)
    shared = {
        "w_qT": np.ascontiguousarray(w_in[0:1024].T),
        "w_kT": np.ascontiguousarray(w_in[1024:2048].T),
        "w_vT": np.ascontiguousarray(w_in[2048:3072].T),
        "w_outT": np.ascontiguousarray(np.asarray(L0_attn_w_out).T),
        "wgT": np.ascontiguousarray(np.asarray(L0_ffn_w_gate).T),
        "wuT": np.ascontiguousarray(np.asarray(L0_ffn_w_up).T),
        "wdT": np.ascontiguousarray(np.asarray(L0_ffn_w_down).T),
        "tri": tri128,
        "ident": np.eye(128, dtype=np.float32),
        "Wdq": np.ascontiguousarray(L1_attn_W_dq),
        "WdqT": np.ascontiguousarray(np.asarray(L1_attn_W_dq).T),
        "WuqT": np.ascontiguousarray(np.asarray(L1_attn_W_uq).T),
        "Wuq": np.ascontiguousarray(L1_attn_W_uq),
        "Wuk": np.ascontiguousarray(L1_attn_W_uk),
        "Wuv": np.ascontiguousarray(L1_attn_W_uv),
        "WoT": np.ascontiguousarray(np.asarray(L1_attn_W_o).T),
        "WdkvT": np.ascontiguousarray(np.asarray(L1_attn_W_dkv).T),
        "WgT_moe": np.ascontiguousarray(np.asarray(L1_ffn_Wg).T),
        "WeT": np.ascontiguousarray(np.asarray(L1_ffn_We).transpose(0, 2, 1)),
        "be": np.ascontiguousarray(L1_ffn_be),
        "sel2_in": np.kron(np.eye(2, dtype=np.float32), np.ones((1, 64), np.float32)),
        "eps_row_in": (np.arange(E, dtype=np.float32) * -EPS_TIE)[None, :],
    }
    mk = {
        0: {"mkA": np.where(tri128 > 0, 0.0, NEG).astype(np.float32),
            "mkB": np.full((128, 128), NEG, np.float32)},
        1: {"mkA": np.zeros((128, 128), np.float32),
            "mkB": np.where(tri128 > 0, 0.0, NEG).astype(np.float32)},
    }
    in_maps = []
    for core in range(8):
        b, u = core // 2, core % 2
        m = dict(shared)
        m["xT"] = np.ascontiguousarray(x[b].T)
        m["u_in"] = np.full((128, 1), float(u), np.float32)
        m.update(mk[u])
        in_maps.append(m)

    if "nc" not in _CACHE:
        nc, p, pt, fin = build(debug=0)
        _close(nc, p, pt, fin)
        _CACHE["nc"] = nc
    nc = _CACHE["nc"]
    res = run_bass_kernel_spmd(nc, in_maps, core_ids=list(range(8)))

    outp = np.empty((B_, T_, D_), np.float32)
    own_idx = {u: (np.arange(16)[:, None] * 256 + u * 128 + np.arange(128)[None, :]).reshape(-1)
               for u in range(2)}
    for core in range(8):
        b, u = core // 2, core % 2
        sh = res.results[core]["out"]
        outp[b, own_idx[u], :] = np.ascontiguousarray(sh.T)
    return outp
